# version 14
# baseline (speedup 1.0000x reference)
import numpy as np
import ml_dtypes
import concourse.bass as bass
import concourse.mybir as mybir
from concourse.bass_utils import run_bass_kernel_spmd

BFD = ml_dtypes.bfloat16
F32 = mybir.dt.float32
BF16 = mybir.dt.bfloat16
AF = mybir.ActivationFunctionType
ALU = mybir.AluOpType
AX = mybir.AxisListType

D = 2048
T = 2048
NCORE = 8
NPT = T // 128
NDT = D // 128

L0H = 32
NB = 16
F0 = 5632
F0TPC = 6
F0P = F0TPC * 128
NF4 = np.array([-1.0, -0.6962890625, -0.52490234375, -0.39453125, -0.284423828125,
                -0.18359375, -0.091064453125, 0.0, 0.07958984375, 0.160888671875,
                0.24609375, 0.337890625, 0.440673828125, 0.5625, 0.72265625, 1.0],
               dtype=np.float64)
NF4_MID = [float(np.float32((NF4[i] + NF4[i + 1]) / 2)) for i in range(15)]
NF4_STEP = [float(np.float32(NF4[i + 1] - NF4[i])) for i in range(15)]
NF4_0 = float(np.float32(NF4[0]))
NBLK_W = 180224.0
L1H = 16
L1KV = 4
RMS_EPS = 1e-8
ROPE_BASE = 10000.0
MASK0 = -3.3895313892515355e38
MASK1 = -10000.0
F1 = 5461
F1TPC = 6
F1P = F1TPC * 128
NELT_T = float(F1 * D)
NFT1 = 43
MAGIC = 12582912.0

ENGS = ("sync", "scalar", "vector", "tensor", "gpsimd")


class Buf:
    __slots__ = ("w", "rs")

    def __init__(self):
        self.w = None
        self.rs = []

    def for_read(self):
        return (self.w,) if self.w else ()

    def for_write(self):
        return tuple(self.rs) + ((self.w,) if self.w else ())

    def add_read(self, tok):
        if tok is not None:
            self.rs.append(tok)

    def set_write(self, tok):
        self.w = tok
        self.rs = []


class P:
    def __init__(self):
        self.ops = {e: [] for e in ENGS}
        self.cnt = {}
        self.waited = {e: {} for e in ENGS}
        self.tags = {e: [] for e in ENGS}

    def _tag(self, eng, kind):
        import traceback
        fs = traceback.extract_stack(limit=6)
        lines = [f.lineno for f in fs if f.filename.endswith("kernel.py")]
        self.tags[eng].append((kind, lines))

    def _wait(self, eng, tok):
        sem, val = tok
        if self.waited[eng].get(sem, 0) >= val:
            return
        self.waited[eng][sem] = val

        def fn(E, S, sem=sem, val=val):
            E.wait_ge(S[sem], val)
        self._tag(eng, f"wait {sem}>={val}")
        self.ops[eng].append(fn)

    def do(self, eng, build, reads=(), writes=(), waits=(), inc=None, inck=1,
           serialize=False):
        seen = []
        if serialize and inc is not None:
            pv = self.cnt.get(inc, 0)
            if pv:
                seen.append((inc, pv))
        for t in waits:
            if t is not None:
                seen.append(t)
        for b in reads:
            seen.extend(b.for_read())
        for b in writes:
            seen.extend(b.for_write())
        for t in seen:
            self._wait(eng, t)
        tok = None
        if inc is not None:
            self.cnt[inc] = self.cnt.get(inc, 0) + inck
            tok = (inc, self.cnt[inc])

        def fn(E, S, build=build, inc=inc, inck=inck):
            inst = build(E, S)
            if inc is not None:
                inst.then_inc(S[inc], inck)
        self._tag(eng, "op")
        self.ops[eng].append(fn)
        for b in reads:
            b.add_read(tok)
        for b in writes:
            b.set_write(tok)
        return tok


def build_nc():
    nc = bass.Bass("TRN2", target_bir_lowering=False)
    p = P()

    def din(name, shape, dt):
        return nc.dram_tensor(name, list(shape), dt, kind="ExternalInput").ap()

    x_in = din("x", (T, D), F32)
    wq0 = din("wq0", (16, 128, D), F32)
    wk0 = din("wk0", (16, 128, D), F32)
    wv0 = din("wv0", (16, 128, D), F32)
    wo0 = din("wo0", (D, D), F32)
    w1s0 = din("w1s0", (F0P, D), BF16)
    w2s0 = din("w2s0", (F0P, D), BF16)
    w3s0 = din("w3s0", (D // 8, F0), BF16)
    wq1 = din("wq1", (16, 128, D), F32)
    wk1 = din("wk1", (4, 128, D), F32)
    wv1 = din("wv1", (4, 128, D), F32)
    wo1 = din("wo1", (D, D), F32)
    bq1 = din("bq1", (1, D), F32)
    bk1 = din("bk1", (1, 512), F32)
    bv1 = din("bv1", (1, 512), F32)
    bo1 = din("bo1", (1, D), F32)
    qn1 = din("qn1", (1, 128), F32)
    kn1 = din("kn1", (1, 128), F32)
    sc1 = din("sc1", (1, 1), F32)
    w1s1 = din("w1s1", (D, F1P), F32)
    w3s1 = din("w3s1", (D, F1P), F32)
    w2s1 = din("w2s1", (F1P, D), F32)

    out = nc.dram_tensor("out", [T, D], F32, kind="ExternalOutput").ap()

    x1d = nc.dram_tensor("x1d", [T, D], F32).ap()
    xpd = nc.dram_tensor("xpd", [T, D], F32).ap()
    NS1 = F0P * 32
    NS3 = (D // 8) * 88
    scf = nc.dram_tensor("scf", [2 * NS1 + NS3], F32).ap()
    scb = nc.dram_tensor("scb", [2 * NS1 + NS3], BF16).ap()
    L0_W1SZ = F0TPC * D * 128
    L0_W3SZ = 44 * 128 * 256
    L0_CON = 2 * L0_W1SZ + L0_W3SZ
    ag0i = nc.dram_tensor("ag0i", [L0_CON], BF16).ap()
    ag0o = nc.dram_tensor("ag0o", [8 * L0_CON], BF16, addr_space="Shared").ap()
    L1_WSZ = F1TPC * D * 128
    L1_CON = 3 * L1_WSZ
    ag1i = nc.dram_tensor("ag1i", [L1_CON], BF16).ap()
    ag1o = nc.dram_tensor("ag1o", [8 * L1_CON], BF16, addr_space="Shared").ap()
    ari = nc.dram_tensor("ari", [1, 6], F32).ap()
    aro = nc.dram_tensor("aro", [1, 6], F32).ap()

    ident = nc.inline_tensor(np.eye(128, dtype=BFD), name="identb").ap()
    r = np.arange(128)[:, None]
    c256 = np.arange(256)[None, :]
    c128 = np.arange(128)[None, :]
    m0a_t = nc.inline_tensor(
        np.where((c256 >= r) & (c256 <= r + 128), 0.0, MASK0).astype(BFD), name="m0a").ap()
    m0b_t = nc.inline_tensor(
        np.where(c128 <= r, 0.0, MASK0).astype(BFD), name="m0b").ap()
    m1a_t = nc.inline_tensor(
        np.where((c256 >= r + 64) & (c256 <= r + 128), 0.0, MASK1).astype(BFD), name="m1a").ap()
    m1b_t = nc.inline_tensor(
        np.where((c128 <= r) & (c128 >= r - 64), 0.0, MASK1).astype(BFD), name="m1b").ap()
    inv = 1.0 / (ROPE_BASE ** (np.arange(0, 128, 2, dtype=np.float64) / 128.0))
    fr = np.arange(T, dtype=np.float64)[:, None] * inv[None, :]
    ch32 = np.cos(fr).astype(np.float32)
    sh32 = np.sin(fr).astype(np.float32)
    cosh_t = nc.inline_tensor(ch32, name="cosh").ap()
    sinh_t = nc.inline_tensor(sh32, name="sinh").ap()
    cosl_t = nc.inline_tensor((np.cos(fr) - ch32).astype(BFD), name="cosl").ap()
    sinl_t = nc.inline_tensor((np.sin(fr) - sh32).astype(BFD), name="sinl").ap()
    ones_t = nc.inline_tensor(np.ones((1, 128), dtype=BFD), name="onesr").ap()
    onescol_t = nc.inline_tensor(np.ones((128, 1), dtype=np.float32), name="onesc").ap()

    import contextlib
    stack = contextlib.ExitStack()

    def sb(name, shape, dt):
        return stack.enter_context(nc.sbuf_tensor(name, list(shape), dt))

    with stack:
        xnT = sb("xnT", (128, NDT, T), BF16)
        arA = sb("arA", (128, 16, T), BF16)
        wsF = sb("wsF", (128, 2, 1024), F32)
        wsB = sb("wsB", (128, 3, 2048), BF16)
        wsC = sb("wsC", (128, 4, 512), BF16)
        xt = sb("xt", (128, 2048), F32)
        xt2 = sb("xt2", (128, 2, 512), F32)
        xnt = sb("xnt", (128, 2048), BF16)
        qTb = sb("qTb", (128, 2048), BF16)
        kTb = sb("kTb", (128, 2048), BF16)
        vb = sb("vb", (128, 16, 128), BF16)
        chT = sb("chT", (128, NPT, 64), F32)
        shT = sb("shT", (128, NPT, 64), F32)
        clT = sb("clT", (128, NPT, 64), BF16)
        slT = sb("slT", (128, NPT, 64), BF16)
        pbb = sb("pbb", (128, 4, 256), BF16)
        pTb = sb("pTb", (128, 4, 2, 128), BF16)
        stat = sb("stat", (128, 28), F32)
        statB = sb("statB", (128, 16), F32)
        s1b = sb("s1b", (128, 560), F32)
        s2b = sb("s2b", (128, 560), BF16)
        sqw = sb("sqw", (128, 256), F32)
        sqw2 = sb("sqw2", (128, 256), F32)
        dsc = arA[:, 13, :]
        dhh = arA[:, 14, :]
        dac = arA[:, 15, :]
        masks = sb("masks", (128, 2, 256), BF16)
        idb = sb("idb", (128, 128), BF16)
        onesb = sb("onesb", (1, 128), BF16)
        onesf = sb("onesf", (1, 128), F32)
        onescol = sb("onescol", (128, 1), F32)
        brow = sb("brow", (1, 512), BF16)
        frow = sb("frow", (1, 128), F32)
        qnp = sb("qnp", (128, 128), F32)
        knp = sb("knp", (128, 128), F32)
        arow = sb("arow", (1, 16), F32)
        cst = sb("cst", (128, 4), F32)

        pp = stack.enter_context(nc.psum_tensor("pp", [128, 4, 512], F32))
        ptr = stack.enter_context(nc.psum_tensor("ptr", [128, 2, 1024], BF16))
        ptr2 = stack.enter_context(nc.psum_tensor("ptr2", [128, 2, 1024], BF16))
        psc = pp
        pav = pp[:, 2:4, :]

        KQ = 12
        sems = {}
        for s in ("cc", "tp", "vp", "ap", "gp"):
            sems[s] = stack.enter_context(nc.semaphore(s))
        for j in range(KQ):
            sems["qS%d" % j] = stack.enter_context(nc.semaphore("qS%d" % j))
            sems["qG%d" % j] = stack.enter_context(nc.semaphore("qG%d" % j))
        qn = {"S": 0, "G": 0}

        B = {}
        for name in ("xnT", "arA", "wsF0", "wsF1", "wsB0", "wsB1", "wsB2", "wsC0", "wsC1",
                     "xt", "xt20", "xt21", "xnt", "qTb", "kTb", "vb", "planes", "wsC2", "wsC3",
                     "pbb0", "pbb1", "pbb2", "pbb3", "pT0", "pT1", "pT2", "pT3", "statLN",
                     "statSM0", "statSM1", "statSM2", "statSM3", "ptr20", "ptr21",
                     "statRMS", "statNF", "statTHR", "statACC", "statB", "s1b", "s2b",
                     "sqw", "sqw2", "dsc", "dhh", "dac", "masks", "idb", "onesb", "onesf",
                     "onescol", "brow", "frow", "qnp", "knp", "arow", "cst",
                     "pp0", "pp1", "pp2", "pp3", "psc0", "psc1", "ptr0", "ptr1",
                     "pav0", "pav1", "xpd", "x1d", "scf", "scb", "ag0i", "ag1i",
                     "ari", "aro"):
            B[name] = Buf()
        PPB = [B["pp0"], B["pp1"], B["pp2"], B["pp3"]]
        PSCB = [B["pp0"], B["pp1"]]
        PTRB = [B["ptr0"], B["ptr1"]]
        PTR2B = [B["ptr20"], B["ptr21"]]
        PAVB = [B["pp2"], B["pp3"]]
        WSFB = [B["wsF0"], B["wsF1"]]
        WSBB = [B["wsB0"], B["wsB1"], B["wsB2"]]
        WSCB = [B["wsC0"], B["wsC1"], B["wsC2"], B["wsC3"]]
        XT2B = [B["xt20"], B["xt21"]]
        B_arA = [Buf() for _ in range(16)]
        PBBB = [B["pbb0"], B["pbb1"], B["pbb2"], B["pbb3"]]
        PTBB = [B["pT0"], B["pT1"], B["pT2"], B["pT3"]]
        SMB = [B["statSM0"], B["statSM1"], B["statSM2"], B["statSM3"]]

        def dmaS(dst, src, reads=(), writes=(), waits=()):
            sem = "qS%d" % (qn["S"] % KQ)
            qn["S"] += 1
            return p.do("sync", lambda E, S, d=dst, s=src: E.dma_start(out=d, in_=s),
                        reads=reads, writes=writes, waits=waits, inc=sem, inck=16,
                        serialize=True)

        def dmaG(dst, src, reads=(), writes=(), waits=()):
            sem = "qG%d" % (qn["G"] % KQ)
            qn["G"] += 1
            return p.do("gpsimd", lambda E, S, d=dst, s=src: E.dma_start(out=d, in_=s),
                        reads=reads, writes=writes, waits=waits, inc=sem, inck=16,
                        serialize=True)

        def vop(build, reads=(), writes=(), waits=()):
            return p.do("vector", build, reads=reads, writes=writes, waits=waits, inc="vp")

        def aop(build, reads=(), writes=(), waits=()):
            return p.do("scalar", build, reads=reads, writes=writes, waits=waits, inc="ap")

        def mm(build, reads=(), writes=(), waits=(), fin=False):
            return p.do("tensor", build, reads=reads, writes=writes, waits=waits,
                        inc="tp" if fin else None)

        class _SB(Exception):
            pass

        import os as _os
        _PH = int(_os.environ.get("PHASE", "99") or "99")

        def _ck(n):
            if _PH < n:
                raise _SB()
        try:
            vop(lambda E, S: E.memset(cst[:, 0:1], 1e-5), writes=(B["cst"],))
            vop(lambda E, S: E.memset(cst[:, 1:2], RMS_EPS), writes=(B["cst"],))
            vop(lambda E, S: E.memset(cst[:, 2:3], 1.0 / 128.0), writes=(B["cst"],))
            vop(lambda E, S: E.memset(onesf[0:1, :], 1.0), writes=(B["onesf"],))
            dmaS(idb[:, :], ident[:, :], writes=(B["idb"],))
            dmaS(masks[:, 0, :], m0a_t[:, :], writes=(B["masks"],))
            dmaS(masks[:, 1, 0:128], m0b_t[:, :], writes=(B["masks"],))
            dmaS(onesb[:, :], ones_t[:, :], writes=(B["onesb"],))
            dmaS(onescol[:, :], onescol_t[:, :], writes=(B["onescol"],))

            vop(lambda E, S: E.memset(stat[:, 0:8], 0.0), writes=(B["statACC"],))

            nf4_tiles = []
            for ti in range(F0TPC):
                nf4_tiles.append((0, w1s0[ti * 128:(ti + 1) * 128, :], ti * 32, 32))
            for ti in range(F0TPC):
                nf4_tiles.append((1, w2s0[ti * 128:(ti + 1) * 128, :], 192 + ti * 32, 32))
            for ptd in range(2):
                for ci, (c0, cw) in enumerate(((0, 2048), (2048, 2048), (4096, 1536))):
                    nf4_tiles.append((2, w3s0[ptd * 128:(ptd + 1) * 128, c0:c0 + cw],
                                      384 + ptd * 88 + (0, 32, 64)[ci], cw // 64))

            for col, src_ap, s1off, nb in nf4_tiles:
                tld = dmaS(dhh[:, 0:nb * 64], src_ap, writes=(B["dhh"],))
                vop(lambda E, S, nb=nb, o=s1off: E.reduce_max(
                    out=s1b[:, o:o + nb],
                    in_=dhh[:, 0:nb * 64].rearrange("p (a b) -> p a b", a=nb),
                    axis=AX.X, apply_absolute_value=True),
                    reads=(B["dhh"],), writes=(B["s1b"],))
                vop(lambda E, S, nb=nb, o=s1off: E.reduce_sum(
                    out=stat[:, 20:21], in_=s1b[:, o:o + nb], axis=AX.X),
                    reads=(B["s1b"],), writes=(B["statNF"],))
                vop(lambda E, S, col=col: E.tensor_add(
                    out=stat[:, col:col + 1], in0=stat[:, col:col + 1], in1=stat[:, 20:21]),
                    reads=(B["statNF"],), writes=(B["statACC"],))

            for col, src, nr, ncc in ((3, w1s1, D, F1P), (4, w3s1, D, F1P), (5, w2s1, F1P, D)):
                for ti in range(nr // 128):
                    tld = dmaS(xt[:, 0:ncc], src[ti * 128:(ti + 1) * 128, :], writes=(B["xt"],))
                    vop(lambda E, S, ncc=ncc: E.reduce_sum(
                        out=stat[:, 8:8 + ncc // 128],
                        in_=xt[:, 0:ncc].rearrange("p (a b) -> p a b", b=128),
                        axis=AX.X, apply_absolute_value=True),
                        reads=(B["xt"],), writes=(B["statNF"],))
                    vop(lambda E, S, ncc=ncc: E.reduce_sum(
                        out=stat[:, 20:21], in_=stat[:, 8:8 + ncc // 128], axis=AX.X),
                        reads=(B["statNF"],), writes=(B["statNF"],))
                    vop(lambda E, S, col=col: E.tensor_add(
                        out=stat[:, col:col + 1], in0=stat[:, col:col + 1], in1=stat[:, 20:21]),
                        reads=(B["statNF"],), writes=(B["statACC"],))

            mm(lambda E, S: E.matmul(pav[0:6, 0, 0:1], stat[:, 0:6], onescol[:, :],
                                     start=True, stop=True),
               reads=(B["statACC"], B["onescol"]), writes=(PAVB[0],), fin=True)
            vop(lambda E, S: E.tensor_copy(out=stat[0:6, 21:22], in_=pav[0:6, 0, 0:1]),
                reads=(PAVB[0],), writes=(B["statNF"],))
            tok_ari = dmaG(ari[0:1, 0:6].rearrange("a b -> (a b)").rearrange("(b c) -> b c", c=1),
                           stat[0:6, 21:22], reads=(B["statNF"],), writes=(B["ari"],))
            tok_cc1 = p.do("gpsimd",
                           lambda E, S: E.collective_compute(
                               "AllReduce", ALU.add, replica_groups=[list(range(8))],
                               ins=[ari.opt()], outs=[aro.opt()]),
                           reads=(B["ari"],), writes=(B["aro"],), inc="cc", serialize=True)
            dmaG(arow[0:1, 0:6], aro[0:1, 0:6], reads=(B["aro"],), writes=(B["arow"],))

            for i, (dst, srcc, mul) in enumerate(
                    [(8, 0, 1.0 / NBLK_W), (9, 1, 1.0 / NBLK_W), (10, 2, 1.0 / NBLK_W),
                     (11, 3, 1.0 / NELT_T), (12, 4, 1.0 / NELT_T), (13, 5, 1.0 / NELT_T)]):
                vop(lambda E, S, dst=dst, srcc=srcc, mul=mul: E.tensor_scalar(
                    out=arow[0:1, dst:dst + 1], in0=arow[0:1, srcc:srcc + 1],
                    scalar1=mul, scalar2=None, op0=ALU.mult),
                    reads=(B["arow"],), writes=(B["arow"],))
            vop(lambda E, S: E.tensor_mul(out=arow[0:1, 14:15], in0=arow[0:1, 12:13],
                                          in1=arow[0:1, 13:14]),
                reads=(B["arow"],), writes=(B["arow"],))
            mm(lambda E, S: E.matmul(pav[:, 0, 0:7], onesf[0:1, :], arow[0:1, 8:15],
                                     start=True, stop=True),
               reads=(B["arow"], B["onesf"]), writes=(PAVB[0],), fin=True)
            vop(lambda E, S: E.tensor_copy(out=statB[:, 0:7], in_=pav[:, 0, 0:7]),
                reads=(PAVB[0],), writes=(B["statB"],))

            def scaler_quant(mcol, s1off, ntile, nbt, nsc, doff, layout3=False):
                if not layout3:
                    dmaG(scf[doff:doff + nsc].rearrange("(t p b) -> p t b", t=ntile, p=128),
                         s1b[:, s1off:s1off + ntile * nbt].rearrange("p (t b) -> p t b", t=ntile),
                         reads=(B["s1b"],), writes=(B["scf"],))
                else:
                    dmaG(scf[doff:doff + nsc].rearrange("(t p b) -> p t b", t=2, p=128),
                         s1b[:, s1off:s1off + 176].rearrange("p (t b) -> p t b", t=2),
                         reads=(B["s1b"],), writes=(B["scf"],))
                nsb = nsc // 256
                dmaS(sqw[0:nsb, :], scf[doff:doff + nsc].rearrange("(q j) -> q j", q=nsb),
                     reads=(B["scf"],), writes=(B["sqw"],))
                vop(lambda E, S, nsb=nsb, mcol=mcol: E.tensor_scalar(
                    out=sqw[0:nsb, :], in0=sqw[0:nsb, :], scalar1=statB[0:nsb, mcol:mcol + 1],
                    scalar2=None, op0=ALU.subtract),
                    reads=(B["statB"],), writes=(B["sqw"],))
                vop(lambda E, S, nsb=nsb: E.reduce_max(
                    out=stat[0:nsb, 17:18], in_=sqw[0:nsb, :], axis=AX.X,
                    apply_absolute_value=True), reads=(B["sqw"],), writes=(B["statNF"],))
                vop(lambda E, S, nsb=nsb: E.reciprocal(out=stat[0:nsb, 18:19],
                                                       in_=stat[0:nsb, 17:18]),
                    reads=(B["statNF"],), writes=(B["statNF"],))
                vop(lambda E, S, nsb=nsb: E.tensor_scalar(
                    out=stat[0:nsb, 18:19], in0=stat[0:nsb, 18:19], scalar1=128.0,
                    scalar2=None, op0=ALU.mult), reads=(B["statNF"],), writes=(B["statNF"],))
                vop(lambda E, S, nsb=nsb: E.tensor_scalar(
                    out=stat[0:nsb, 19:20], in0=stat[0:nsb, 17:18], scalar1=1.0 / 128.0,
                    scalar2=None, op0=ALU.mult), reads=(B["statNF"],), writes=(B["statNF"],))
                vop(lambda E, S, nsb=nsb: E.tensor_scalar(
                    out=sqw2[0:nsb, :], in0=sqw[0:nsb, :], scalar1=stat[0:nsb, 18:19],
                    scalar2=MAGIC, op0=ALU.mult, op1=ALU.add),
                    reads=(B["sqw"], B["statNF"]), writes=(B["sqw2"],))
                vop(lambda E, S, nsb=nsb: E.tensor_scalar(
                    out=sqw2[0:nsb, :], in0=sqw2[0:nsb, :], scalar1=MAGIC,
                    scalar2=None, op0=ALU.subtract),
                    writes=(B["sqw2"],))
                vop(lambda E, S, nsb=nsb: E.tensor_scalar(
                    out=sqw2[0:nsb, :], in0=sqw2[0:nsb, :], scalar1=127.0, scalar2=-128.0,
                    op0=ALU.min, op1=ALU.max), writes=(B["sqw2"],))
                vop(lambda E, S, nsb=nsb: E.tensor_scalar(
                    out=sqw[0:nsb, :], in0=sqw2[0:nsb, :], scalar1=stat[0:nsb, 19:20],
                    scalar2=None, op0=ALU.mult),
                    reads=(B["sqw2"], B["statNF"]), writes=(B["sqw"],))
                vop(lambda E, S, nsb=nsb: E.tensor_copy(out=dsc[0:nsb, 0:256],
                                                        in_=sqw[0:nsb, :]),
                    reads=(B["sqw"],), writes=(B["dsc"],))
                vop(lambda E, S, nsb=nsb, mcol=mcol: E.tensor_scalar(
                    out=dsc[0:nsb, 0:256], in0=dsc[0:nsb, 0:256],
                    scalar1=statB[0:nsb, mcol:mcol + 1], scalar2=None, op0=ALU.add),
                    reads=(B["statB"],), writes=(B["dsc"],))
                dmaG(scb[doff:doff + nsc].rearrange("(q j) -> q j", q=nsb), dsc[0:nsb, 0:256],
                     reads=(B["dsc"],), writes=(B["scb"],))
                if not layout3:
                    dmaS(s2b[:, s1off:s1off + ntile * nbt].rearrange("p (t b) -> p t b", t=ntile),
                         scb[doff:doff + nsc].rearrange("(t p b) -> p t b", t=ntile, p=128),
                         reads=(B["scb"],), writes=(B["s2b"],))
                else:
                    dmaS(s2b[:, s1off:s1off + 176].rearrange("p (t b) -> p t b", t=2),
                         scb[doff:doff + nsc].rearrange("(t p b) -> p t b", t=2, p=128),
                         reads=(B["scb"],), writes=(B["s2b"],))

            p0_units = []
            agtok = {}
            _ck(2)
            p0_units.append(lambda: scaler_quant(0, 0, F0TPC, 32, NS1, 0))
            p0_units.append(lambda: scaler_quant(1, 192, F0TPC, 32, NS1, NS1))
            p0_units.append(lambda: scaler_quant(2, 384, 2, 88, NS3, 2 * NS1,
                                                 layout3=True))

            def nf4_dequant_tile(src_ap, s1off, nb):
                dmaS(dhh[:, 0:nb * 64], src_ap, writes=(B["dhh"],))
                vop(lambda E, S, o=s1off, nb=nb: E.tensor_scalar(
                    out=s1b[:, o:o + nb], in0=s1b[:, o:o + nb], scalar1=1e-30,
                    scalar2=None, op0=ALU.max), writes=(B["s1b"],))
                vop(lambda E, S, o=s1off, nb=nb: E.reciprocal(
                    out=sqw2[:, 0:nb], in_=s1b[:, o:o + nb]),
                    reads=(B["s1b"],), writes=(B["sqw2"],))
                vop(lambda E, S, nb=nb: E.tensor_tensor(
                    out=dsc[:, 0:nb * 64].rearrange("p (a b) -> p a b", a=nb),
                    in0=dhh[:, 0:nb * 64].rearrange("p (a b) -> p a b", a=nb),
                    in1=sqw2[:, 0:nb].rearrange("p (a u) -> p a u", u=1)
                        .broadcast_to([128, nb, 64]),
                    op=ALU.mult), reads=(B["dhh"], B["sqw2"]), writes=(B["dsc"],))
                vop(lambda E, S, nb=nb: E.memset(dac[:, 0:nb * 64], NF4_0), writes=(B["dac"],))
                for j in range(15):
                    vop(lambda E, S, nb=nb, j=j: E.tensor_scalar(
                        out=dhh[:, 0:nb * 64], in0=dsc[:, 0:nb * 64], scalar1=NF4_MID[j],
                        scalar2=None, op0=ALU.is_gt), reads=(B["dsc"],), writes=(B["dhh"],))
                    vop(lambda E, S, nb=nb, j=j: E.scalar_tensor_tensor(
                        out=dac[:, 0:nb * 64], in0=dhh[:, 0:nb * 64], scalar=NF4_STEP[j],
                        in1=dac[:, 0:nb * 64], op0=ALU.mult, op1=ALU.add),
                        reads=(B["dhh"],), writes=(B["dac"],))
                return vop(lambda E, S, o=s1off, nb=nb: E.tensor_tensor(
                    out=dsc[:, 0:nb * 64].rearrange("p (a b) -> p a b", a=nb),
                    in0=dac[:, 0:nb * 64].rearrange("p (a b) -> p a b", a=nb),
                    in1=s2b[:, o:o + nb].rearrange("p (a u) -> p a u", u=1)
                        .broadcast_to([128, nb, 64]),
                    op=ALU.mult), reads=(B["dac"], B["s2b"]), writes=(B["dsc"],))

            def transpose_store(nchunks, dst_fn):
                for j in range(nchunks):
                    mm(lambda E, S, j=j: E.transpose(ptr2[:, j % 2, 0:128],
                                                     dsc[:, j * 128:(j + 1) * 128], idb[:, :]),
                       reads=(B["dsc"], B["idb"]), writes=(PTR2B[j % 2],), fin=True)
                    aop(lambda E, S, j=j: E.copy(
                        out=xnt[:, (j % 2) * 128:(j % 2) * 128 + 128], in_=ptr2[:, j % 2, 0:128]),
                        reads=(PTR2B[j % 2],), writes=(B["xnt"],))
                    dmaG(dst_fn(j), xnt[:, (j % 2) * 128:(j % 2) * 128 + 128],
                         reads=(B["xnt"],), writes=(B["ag0i"],))

            _ck(3)
            def u_nf4(wslot, src, ti):
                cb = wslot * L0_W1SZ
                nf4_dequant_tile(src[ti * 128:(ti + 1) * 128, :], wslot * 192 + ti * 32, 32)
                transpose_store(NDT, lambda j, cb=cb, ti=ti: ag0i[
                    cb + ti * (D * 128):cb + (ti + 1) * (D * 128)]
                    .rearrange("(p a f) -> p a f", p=128, a=NDT)[:, j, :])

            def u_nf4w3(ptd, ci):
                cb = 2 * L0_W1SZ
                c0, cw = ((0, 2048), (2048, 2048), (4096, 1536))[ci]
                nf4_dequant_tile(w3s0[ptd * 128:(ptd + 1) * 128, c0:c0 + cw],
                                 384 + ptd * 88 + (0, 32, 64)[ci], cw // 64)
                transpose_store(cw // 128, lambda j, cb=cb, c0=c0, ptd=ptd: ag0i[
                    cb + ((c0 // 128) + j) * (128 * 256):
                    cb + ((c0 // 128) + j + 1) * (128 * 256)]
                    .rearrange("(p f) -> p f", p=128)[:, ptd * 128:(ptd + 1) * 128])

            for wslot, src in ((0, w1s0), (1, w2s0)):
                for ti in range(F0TPC):
                    p0_units.append(lambda wslot=wslot, src=src, ti=ti: u_nf4(wslot, src, ti))
            for ptd in range(2):
                for ci in range(3):
                    p0_units.append(lambda ptd=ptd, ci=ci: u_nf4w3(ptd, ci))

            def u_ag0():
                agtok["ag0"] = p.do(
                    "gpsimd",
                    lambda E, S: E.collective_compute(
                        "AllGather", ALU.bypass, replica_groups=[list(range(8))],
                        ins=[ag0i.opt()], outs=[ag0o.opt()]),
                    reads=(B["ag0i"],), waits=(B["ag0i"].w,), inc="cc", serialize=True)
            p0_units.append(u_ag0)

            _ck(4)
            def u_tern(slot, src, scol, mode):
                cb = slot * L1_WSZ
                vop(lambda E, S, scol=scol: E.tensor_scalar(
                    out=stat[:, 22:23], in0=statB[:, scol:scol + 1], scalar1=0.5,
                    scalar2=None, op0=ALU.mult), reads=(B["statB"],), writes=(B["statTHR"],))
                vop(lambda E, S, scol=scol: E.tensor_scalar(
                    out=stat[:, 23:24], in0=statB[:, scol:scol + 1], scalar1=-0.5,
                    scalar2=None, op0=ALU.mult), reads=(B["statB"],), writes=(B["statTHR"],))
                nrt = (D if mode == "col" else F1P) // 128
                ncc = F1P if mode == "col" else D
                for ti in range(nrt):
                    dmaS(xt[:, 0:ncc], src[ti * 128:(ti + 1) * 128, :], writes=(B["xt"],))
                    vop(lambda E, S, ncc=ncc: E.tensor_scalar(
                        out=xnt[:, 0:ncc], in0=xt[:, 0:ncc], scalar1=stat[:, 22:23],
                        scalar2=1.0, op0=ALU.is_gt, op1=ALU.subtract),
                        reads=(B["xt"], B["statTHR"]), writes=(B["xnt"],))
                    tt = vop(lambda E, S, ncc=ncc: E.scalar_tensor_tensor(
                        out=dac[:, 0:ncc], in0=xt[:, 0:ncc], scalar=stat[:, 23:24],
                        in1=xnt[:, 0:ncc], op0=ALU.is_ge, op1=ALU.add),
                        reads=(B["xt"], B["statTHR"], B["xnt"]), writes=(B["dac"],))
                    if mode == "col":
                        for j in range(F1TPC):
                            dmaG(ag1i[cb + j * (D * 128):cb + (j + 1) * (D * 128)]
                                 .rearrange("(p a f) -> p a f", p=128, a=NDT)[:, ti, :],
                                 dac[:, j * 128:(j + 1) * 128],
                                 reads=(B["dac"],), writes=(B["ag1i"],))
                    else:
                        dmaG(ag1i[cb + ti * (128 * D):cb + (ti + 1) * (128 * D)]
                             .rearrange("(p f) -> p f", p=128), dac[:, 0:D],
                             reads=(B["dac"],), writes=(B["ag1i"],))

            for _sl, _sr, _sc, _md in ((0, w1s1, 3, "col"), (1, w3s1, 4, "col"),
                                       (2, w2s1, 5, "row")):
                p0_units.append(lambda a=_sl, b=_sr, c=_sc, d=_md: u_tern(a, b, c, d))

            def u_ag1():
                agtok["ag1"] = p.do(
                    "gpsimd",
                    lambda E, S: E.collective_compute(
                        "AllGather", ALU.bypass, replica_groups=[list(range(8))],
                        ins=[ag1i.opt()], outs=[ag1o.opt()]),
                    reads=(B["ag1i"],), waits=(B["ag1i"].w,), inc="cc", serialize=True)
                merge_scratch_planes()
            p0_units.append(u_ag1)

            def merge_scratch_planes():
                for _nm, _pl in (("dsc", 13), ("dhh", 14), ("dac", 15)):
                    _b = B[_nm]
                    B_arA[_pl].rs.extend(_b.rs)
                    if _b.w:
                        B_arA[_pl].rs.append(_b.w)
                    _b.w = None
                    _b.rs = []

            def layernorm_pass(x_src, extra=()):
                for pt in range(NPT):
                    dmaS(xt[:, :], x_src[pt * 128:(pt + 1) * 128, :], writes=(B["xt"],),
                         waits=extra if pt == 0 else ())
                    vop(lambda E, S: E.reduce_sum(out=stat[:, 0:1], in_=xt[:, :], axis=AX.X),
                        reads=(B["xt"],), writes=(B["statLN"],))
                    aop(lambda E, S: E.activation(out=xnt[:, :], in_=xt[:, :], func=AF.Square,
                                                  accum_out=stat[:, 1:2]),
                        reads=(B["xt"],), writes=(B["xnt"], B["statLN"]))
                    vop(lambda E, S: E.tensor_scalar(out=stat[:, 2:3], in0=stat[:, 0:1],
                                                     scalar1=1.0 / D, scalar2=None,
                                                     op0=ALU.mult),
                        reads=(B["statLN"],), writes=(B["statLN"],))
                    vop(lambda E, S: E.tensor_scalar(out=stat[:, 3:4], in0=stat[:, 1:2],
                                                     scalar1=1.0 / D, scalar2=None,
                                                     op0=ALU.mult),
                        reads=(B["statLN"],), writes=(B["statLN"],))
                    vop(lambda E, S: E.tensor_scalar(out=stat[:, 4:5], in0=stat[:, 2:3],
                                                     scalar1=stat[:, 2:3], scalar2=None,
                                                     op0=ALU.mult),
                        reads=(B["statLN"],), writes=(B["statLN"],))
                    vop(lambda E, S: E.tensor_sub(out=stat[:, 3:4], in0=stat[:, 3:4],
                                                  in1=stat[:, 4:5]),
                        reads=(B["statLN"],), writes=(B["statLN"],))
                    aop(lambda E, S: E.activation(out=stat[:, 5:6], in_=stat[:, 3:4],
                                                  func=AF.Sqrt, bias=cst[:, 0:1]),
                        reads=(B["statLN"], B["cst"]), writes=(B["statLN"],))
                    vop(lambda E, S: E.reciprocal(out=stat[:, 6:7], in_=stat[:, 5:6]),
                        reads=(B["statLN"],), writes=(B["statLN"],))
                    vop(lambda E, S: E.tensor_scalar(out=xnt[:, :], in0=xt[:, :],
                                                     scalar1=stat[:, 2:3], scalar2=stat[:, 6:7],
                                                     op0=ALU.subtract, op1=ALU.mult),
                        reads=(B["xt"], B["statLN"]), writes=(B["xnt"],))
                    for j in range(NDT):
                        mm(lambda E, S, j=j: E.transpose(
                            ptr2[:, j % 2, 0:128], xnt[:, j * 128:(j + 1) * 128], idb[:, :]),
                           reads=(B["xnt"], B["idb"]), writes=(PTR2B[j % 2],), fin=True)
                        aop(lambda E, S, j=j, pt=pt: E.copy(
                            out=xnT[:, j, pt * 128:(pt + 1) * 128], in_=ptr2[:, j % 2, 0:128]),
                            reads=(PTR2B[j % 2],), writes=(B["xnT"],))

            def stream_w(dram_ap, hg, slot):
                view = wsB[:, slot, :].rearrange("p (a b) -> p a b", a=NDT)
                for half in range(2):
                    (dmaS if half == 0 else dmaG)(wsF[:, half, :],
                         dram_ap[hg, :, half * 1024:(half + 1) * 1024],
                         writes=(WSFB[half],))
                    aop(lambda E, S, half=half, slot=slot: E.copy(
                        out=wsB[:, slot, half * 1024:(half + 1) * 1024],
                        in_=wsF[:, half, :]),
                        reads=(WSFB[half],), writes=(WSBB[slot],))
                return view

            _ck(5)
            layernorm_pass(x_in)

            avT = arA

            def proj_dk(view, slot, dst, dstb):
                for tc in range(4):
                    for z in range(NDT):
                        mm(lambda E, S, z=z, tc=tc, view=view: E.matmul(
                            pp[:, tc, :], view[:, z, :], xnT[:, z, tc * 512:(tc + 1) * 512],
                            start=(z == 0), stop=(z == NDT - 1)),
                           reads=(WSBB[slot], B["xnT"]), writes=(PPB[tc],),
                           fin=(z == NDT - 1))
                    vop(lambda E, S, tc=tc, dst=dst: E.tensor_copy(
                        out=dst[:, tc * 512:(tc + 1) * 512], in_=pp[:, tc, :]),
                        reads=(PPB[tc],), writes=(dstb,))

            _ck(6)
            for hg in range(16):
                if hg < 12:
                    for _ in range(2):
                        if p0_units:
                            p0_units.pop(0)()
                else:
                    while p0_units:
                        p0_units.pop(0)()
                vq = stream_w(wq0, hg, 0)
                proj_dk(vq, 0, qTb, B["qTb"])
                vk = stream_w(wk0, hg, 1)
                proj_dk(vk, 1, kTb, B["kTb"])
                vv = stream_w(wv0, hg, 2)
                for pt in range(NPT):
                    bank = pt % 4
                    for z in range(NDT):
                        mm(lambda E, S, z=z, pt=pt, bank=bank, vv=vv: E.matmul(
                            pp[:, bank, 0:128], xnT[:, z, pt * 128:(pt + 1) * 128],
                            vv[:, z, :], start=(z == 0), stop=(z == NDT - 1)),
                           reads=(WSBB[2], B["xnT"]), writes=(PPB[bank],),
                           fin=(z == NDT - 1))
                    vop(lambda E, S, pt=pt, bank=bank: E.tensor_copy(
                        out=vb[:, pt, :], in_=pp[:, bank, 0:128]),
                        reads=(PPB[bank],), writes=(B["vb"],))

                for hh in range(2):
                    off = 64 * hh

                    def sc_part(i, off=off, hg=hg):
                        par = i % 2
                        j = i % 4
                        nk = 128 if i == 0 else 256
                        if i == 0:
                            mm(lambda E, S, off=off: E.matmul(
                                psc[:, 0, 0:128], qTb[off:off + 64, 0:128],
                                kTb[off:off + 64, 0:128], start=True, stop=True),
                               reads=(B["qTb"], B["kTb"]), writes=(PSCB[0],), fin=True)
                        else:
                            mm(lambda E, S, off=off, i=i, par=par: E.matmul(
                                psc[:, par, 0:256], qTb[off:off + 64, i * 128:(i + 1) * 128],
                                kTb[off:off + 64, (i - 1) * 128:(i + 1) * 128],
                                start=True, stop=True),
                               reads=(B["qTb"], B["kTb"]), writes=(PSCB[par],), fin=True)
                        mrow = masks[:, 0, :] if i > 0 else masks[:, 1, 0:128]
                        vop(lambda E, S, par=par, nk=nk, mrow=mrow: E.scalar_tensor_tensor(
                            out=psc[:, par, 0:nk], in0=psc[:, par, 0:nk], scalar=0.125,
                            in1=mrow, op0=ALU.mult, op1=ALU.add),
                            reads=(B["masks"],), writes=(PSCB[par],))
                        aop(lambda E, S, par=par, j=j, nk=nk: E.activation(
                            out=pbb[:, j, 0:nk], in_=psc[:, par, 0:nk], func=AF.Exp,
                            accum_out=stat[:, 7 + j:8 + j]),
                            reads=(PSCB[par],), writes=(PBBB[j], SMB[j]))
                        vop(lambda E, S, j=j: E.reciprocal(
                            out=stat[:, 24 + j:25 + j], in_=stat[:, 7 + j:8 + j]),
                            reads=(SMB[j],), writes=(SMB[j],))
                        vop(lambda E, S, j=j, nk=nk: E.tensor_scalar(
                            out=pbb[:, j, 0:nk], in0=pbb[:, j, 0:nk],
                            scalar1=stat[:, 24 + j:25 + j], scalar2=None, op0=ALU.mult),
                            reads=(SMB[j],), writes=(PBBB[j],))

                    def tr_part(i, off=off, hg=hg):
                        j = i % 4
                        nk = 128 if i == 0 else 256
                        nkt = nk // 128
                        for kk in range(nkt):
                            mm(lambda E, S, j=j, kk=kk: E.transpose(
                                ptr[:, kk, 0:128], pbb[:, j, kk * 128:(kk + 1) * 128],
                                idb[:, :]),
                               reads=(PBBB[j], B["idb"]), writes=(PTRB[kk],), fin=True)
                            vop(lambda E, S, j=j, kk=kk: E.tensor_copy(
                                out=pTb[:, j, kk, :], in_=ptr[:, kk, 0:128]),
                                reads=(PTRB[kk],), writes=(PTBB[j],))

                    def mv_part(i, off=off, hg=hg):
                        par = i % 2
                        j = i % 4
                        nk = 128 if i == 0 else 256
                        nkt = nk // 128
                        for kk in range(nkt):
                            kpt = (i - 1 + kk) if i > 0 else 0
                            tpos = (0, off) if off else None
                            mm(lambda E, S, kpt=kpt, par=par, j=j, kk=kk, off=off,
                               tpos=tpos, st=(kk == 0), sp=(kk == nkt - 1): E.matmul(
                                pav[off:off + 64, par, 0:128], vb[:, kpt, off:off + 64],
                                pTb[:, j, kk, :], start=st, stop=sp, tile_position=tpos),
                               reads=(B["vb"], PTBB[j]), writes=(PAVB[par],),
                               fin=(kk == nkt - 1))
                        vop(lambda E, S, par=par, off=off, hg=hg, i=i: E.tensor_copy(
                            out=avT[off:off + 64, hg, i * 128:(i + 1) * 128],
                            in_=pav[off:off + 64, par, 0:128]),
                            reads=(PAVB[par],), writes=(B_arA[hg],))

                    sc_part(0)
                    sc_part(1)
                    tr_part(0)
                    for i in range(NB):
                        if i + 2 < NB:
                            sc_part(i + 2)
                        if i + 1 < NB:
                            tr_part(i + 1)
                        mv_part(i)


            def y_project(wT_dram, src_resid, dst_dram, dstB, bias_dram, scale_col):
                for dc in range(4):
                    if bias_dram is not None:
                        dmaG(brow[0:1, 0:512], bias_dram[0:1, dc * 512:(dc + 1) * 512],
                             writes=(B["brow"],))
                    for ptg in range(4):
                        for k in range(NDT):
                            parf = k % 2
                            par = k % 4
                            (dmaS if parf == 0 else dmaG)(wsF[:, parf, 0:512],
                                 wT_dram[k * 128:(k + 1) * 128, dc * 512:(dc + 1) * 512],
                                 writes=(WSFB[parf],))
                            aop(lambda E, S, par=par, parf=parf: E.copy(
                                out=wsC[:, par, :], in_=wsF[:, parf, 0:512]),
                                reads=(WSFB[parf],), writes=(WSCB[par],))
                            for ptl in range(4):
                                pt = ptg * 4 + ptl
                                last = (k == NDT - 1) and bias_dram is None
                                mm(lambda E, S, k=k, pt=pt, ptl=ptl, par=par,
                                   st=(k == 0), sp=last: E.matmul(
                                    pp[:, ptl, :], avT[:, k, pt * 128:(pt + 1) * 128],
                                    wsC[:, par, :], start=st, stop=sp),
                                   reads=(B_arA[k], WSCB[par]), writes=(PPB[ptl],),
                                   fin=(ptl == 3 or last))
                        if bias_dram is not None:
                            for ptl in range(4):
                                mm(lambda E, S, ptl=ptl: E.matmul(
                                    pp[:, ptl, :], onesb[0:1, :], brow[0:1, 0:512],
                                    start=False, stop=True),
                                   reads=(B["onesb"], B["brow"]), writes=(PPB[ptl],),
                                   fin=True)
                        for ptl in range(4):
                            pt = ptg * 4 + ptl
                            xb = ptl % 2
                            dmaS(xt2[:, xb, :],
                                 src_resid[pt * 128:(pt + 1) * 128, dc * 512:(dc + 1) * 512],
                                 writes=(XT2B[xb],))
                            vop(lambda E, S, ptl=ptl, xb=xb: E.tensor_add(
                                out=xt2[:, xb, :], in0=pp[:, ptl, :], in1=xt2[:, xb, :]),
                                reads=(PPB[ptl],), writes=(XT2B[xb],))
                            dmaG(dst_dram[pt * 128:(pt + 1) * 128, dc * 512:(dc + 1) * 512],
                                 xt2[:, xb, :], reads=(XT2B[xb],), writes=(dstB,))

            y_project(wo0, x_in, xpd, B["xpd"], None, None)

            gview = arA[:, :, :].rearrange("p a b -> p (a b)")

            def ffn(ag_w_fn, ag_rhs_fn, nft, resid_dram, dst_dram, dstB, silu_scale,
                    out_scale, ag_tok):
                for tc in range(4):
                    for ft in range(nft):
                        fw = 128 if (nft == 44 or ft < 42) else F1 - 42 * 128
                        sa = (2 * ft) % 3
                        sb_ = (2 * ft + 1) % 3
                        dmaS(wsB[:, sa, :], ag_w_fn(0, ft), writes=(WSBB[sa],),
                             waits=(ag_tok,))
                        dmaG(wsB[:, sb_, :], ag_w_fn(1, ft), writes=(WSBB[sb_],),
                             waits=(ag_tok,))
                        va = wsB[:, sa, :].rearrange("p (a b) -> p a b", a=NDT)
                        vb_ = wsB[:, sb_, :].rearrange("p (a b) -> p a b", a=NDT)
                        for z in range(NDT):
                            mm(lambda E, S, z=z, tc=tc, fw=fw, v1=va: E.matmul(
                                pp[0:fw, 0, :], v1[:, z, 0:fw],
                                xnT[:, z, tc * 512:(tc + 1) * 512],
                                start=(z == 0), stop=(z == NDT - 1)),
                               reads=(WSBB[sa], B["xnT"]), writes=(PPB[0],),
                               fin=(z == NDT - 1))
                        if silu_scale is None:
                            aop(lambda E, S, fw=fw: E.activation(
                                out=xnt[0:fw, 0:512], in_=pp[0:fw, 0, :], func=AF.Silu),
                                reads=(PPB[0],), writes=(B["xnt"],))
                        else:
                            aop(lambda E, S, fw=fw: E.activation(
                                out=xnt[0:fw, 0:512], in_=pp[0:fw, 0, :], func=AF.Silu,
                                scale=silu_scale[0:fw]),
                                reads=(PPB[0], B["statB"]), writes=(B["xnt"],))
                        for z in range(NDT):
                            mm(lambda E, S, z=z, tc=tc, fw=fw, v1=vb_: E.matmul(
                                pp[0:fw, 1, :], v1[:, z, 0:fw],
                                xnT[:, z, tc * 512:(tc + 1) * 512],
                                start=(z == 0), stop=(z == NDT - 1)),
                               reads=(WSBB[sb_], B["xnT"]), writes=(PPB[1],),
                               fin=(z == NDT - 1))
                        vop(lambda E, S, ft=ft, fw=fw: E.tensor_tensor(
                            out=gview[0:fw, ft * 512:(ft + 1) * 512], in0=xnt[0:fw, 0:512],
                            in1=pp[0:fw, 1, :], op=ALU.mult),
                            reads=(B["xnt"], PPB[1]), writes=(B_arA[ft // 4],))
                    for dc in range(4):
                        for fg in range(0, nft, 4):
                            fts = list(range(fg, min(fg + 4, nft)))
                            dmaF = dmaS if (fg // 4) % 2 == 0 else dmaG
                            for ft in fts:
                                pn, srcs = ag_rhs_fn(ft, dc)
                                for dst_sl, src_ap in srcs:
                                    dmaF(dst_sl, src_ap, writes=(WSCB[ft % 4],),
                                         waits=(ag_tok,))
                            for ft in fts:
                                fw = 128 if (nft == 44 or ft < 42) else F1 - 42 * 128
                                par = ft % 4
                                for ptl in range(4):
                                    grp_last = (ft == fts[-1] and ptl == 3)
                                    rd = ((B_arA[fg // 4],) + tuple(WSCB[f % 4] for f in fts)
                                          if grp_last else (B_arA[fg // 4], WSCB[par]))
                                    mm(lambda E, S, ft=ft, ptl=ptl, fw=fw, par=par,
                                       st=(ft == 0), sp=(ft == nft - 1): E.matmul(
                                        pp[:, ptl, :],
                                        gview[0:fw, ft * 512 + ptl * 128:ft * 512 + ptl * 128 + 128],
                                        wsC[0:fw, par, :], start=st, stop=sp),
                                       reads=rd, writes=(PPB[ptl],),
                                       fin=(grp_last or ft == nft - 1))
                        for ptl in range(4):
                            pt = tc * 4 + ptl
                            xb = ptl % 2
                            dmaS(xt2[:, xb, :],
                                 resid_dram[pt * 128:(pt + 1) * 128, dc * 512:(dc + 1) * 512],
                                 reads=(B["xpd"],), writes=(XT2B[xb],))
                            if out_scale is None:
                                vop(lambda E, S, ptl=ptl, xb=xb: E.tensor_add(
                                    out=xt2[:, xb, :], in0=pp[:, ptl, :], in1=xt2[:, xb, :]),
                                    reads=(PPB[ptl],), writes=(XT2B[xb],))
                            else:
                                vop(lambda E, S, ptl=ptl, xb=xb: E.scalar_tensor_tensor(
                                    out=xt2[:, xb, :], in0=pp[:, ptl, :], scalar=out_scale,
                                    in1=xt2[:, xb, :], op0=ALU.mult, op1=ALU.add),
                                    reads=(PPB[ptl], B["statB"]), writes=(XT2B[xb],))
                            dmaG(dst_dram[pt * 128:(pt + 1) * 128, dc * 512:(dc + 1) * 512],
                                 xt2[:, xb, :], reads=(XT2B[xb],), writes=(dstB,))

            def ag0_w(ws, ft):
                sh, lt = ft // F0TPC, ft % F0TPC
                off = sh * L0_CON + ws * L0_W1SZ + lt * (D * 128)
                return ag0o[off:off + D * 128].rearrange("(p f) -> p f", p=128)

            def ag0_rhs(ft, dc):
                res = []
                for half in range(2):
                    sh = dc * 2 + half
                    off = sh * L0_CON + 2 * L0_W1SZ + ft * (128 * 256)
                    res.append((wsC[:, ft % 4, half * 256:(half + 1) * 256],
                                ag0o[off:off + 128 * 256].rearrange("(p f) -> p f", p=128)))
                return 1, res

            _ck(7)
            ffn(ag0_w, ag0_rhs, 44, xpd, x1d, B["x1d"], None, None, agtok["ag0"])

            _ck(8)
            layernorm_pass(x1d, extra=(B["x1d"].w,))

            _ck(9)
            dmaS(masks[:, 0, :], m1a_t[:, :], writes=(B["masks"],))
            dmaS(masks[:, 1, 0:128], m1b_t[:, :], writes=(B["masks"],))
            dmaS(chT[:, :, :], cosh_t[:, :].rearrange("(a p) f -> p a f", p=128),
                 writes=(B["planes"],))
            dmaS(shT[:, :, :], sinh_t[:, :].rearrange("(a p) f -> p a f", p=128),
                 writes=(B["planes"],))
            dmaS(clT[:, :, :], cosl_t[:, :].rearrange("(a p) f -> p a f", p=128),
                 writes=(B["planes"],))
            dmaS(slT[:, :, :], sinl_t[:, :].rearrange("(a p) f -> p a f", p=128),
                 writes=(B["planes"],))
            dmaS(frow[0:1, :], qn1[0:1, :], writes=(B["frow"],))
            mm(lambda E, S: E.matmul(pav[:, 0, 0:128], onesf[0:1, :], frow[0:1, :],
                                     start=True, stop=True),
               reads=(B["frow"], B["onesf"]), writes=(PAVB[0],), fin=True)
            vop(lambda E, S: E.tensor_copy(out=qnp[:, :], in_=pav[:, 0, 0:128]),
                reads=(PAVB[0],), writes=(B["qnp"],))
            dmaS(frow[0:1, :], kn1[0:1, :], writes=(B["frow"],))
            mm(lambda E, S: E.matmul(pav[:, 1, 0:128], onesf[0:1, :], frow[0:1, :],
                                     start=True, stop=True),
               reads=(B["frow"], B["onesf"]), writes=(PAVB[1],), fin=True)
            vop(lambda E, S: E.tensor_copy(out=knp[:, :], in_=pav[:, 1, 0:128]),
                reads=(PAVB[1],), writes=(B["knp"],))
            dmaS(arow[0:1, 15:16], sc1[0:1, :], writes=(B["arow"],))
            mm(lambda E, S: E.matmul(pav[:, 0, 0:1], onesf[0:1, :], arow[0:1, 15:16],
                                     start=True, stop=True),
               reads=(B["arow"], B["onesf"]), writes=(PAVB[0],), fin=True)
            vop(lambda E, S: E.tensor_copy(out=statB[:, 8:9], in_=pav[:, 0, 0:1]),
                reads=(PAVB[0],), writes=(B["statB"],))

            qtok = xt.rearrange("p (a b) -> p a b", a=NPT)
            rview = xnt.rearrange("p (a b) -> p a b", a=NPT)

            def l1_head(wT, b_dram, hgi, npl, nplB, dstT, dstB):
                col0 = hgi * 128
                view = stream_w(wT, hgi, 0)
                dmaG(brow[0:1, 0:128], b_dram[0:1, col0:col0 + 128], writes=(B["brow"],))
                for pt in range(NPT):
                    bank = pt % 4
                    for z in range(NDT):
                        mm(lambda E, S, z=z, pt=pt, bank=bank, view=view: E.matmul(
                            pp[:, bank, 0:128], xnT[:, z, pt * 128:(pt + 1) * 128],
                            view[:, z, :], start=(z == 0), stop=False),
                           reads=(WSBB[0], B["xnT"]), writes=(PPB[bank],))
                    mm(lambda E, S, bank=bank: E.matmul(
                        pp[:, bank, 0:128], onesb[0:1, :], brow[0:1, 0:128],
                        start=False, stop=True),
                       reads=(B["onesb"], B["brow"]), writes=(PPB[bank],), fin=True)
                    vop(lambda E, S, pt=pt, bank=bank: E.tensor_copy(
                        out=qtok[:, pt, :], in_=pp[:, bank, 0:128]),
                        reads=(PPB[bank],), writes=(B["xt"],))
                    aop(lambda E, S, pt=pt: E.activation(
                        out=rview[:, pt, :], in_=qtok[:, pt, :], func=AF.Square,
                        accum_out=stat[:, 14:15]),
                        reads=(B["xt"],), writes=(B["xnt"], B["statRMS"]))
                    aop(lambda E, S: E.activation(out=stat[:, 15:16], in_=stat[:, 14:15],
                                                  func=AF.Sqrt, scale=cst[:, 2:3],
                                                  bias=cst[:, 1:2]),
                        reads=(B["statRMS"], B["cst"]), writes=(B["statRMS"],))
                    vop(lambda E, S: E.reciprocal(out=stat[:, 16:17], in_=stat[:, 15:16]),
                        reads=(B["statRMS"],), writes=(B["statRMS"],))
                    vop(lambda E, S, pt=pt, npl=npl: E.scalar_tensor_tensor(
                        out=qtok[:, pt, :], in0=qtok[:, pt, :], scalar=stat[:, 16:17],
                        in1=npl, op0=ALU.mult, op1=ALU.mult),
                        reads=(B["statRMS"], nplB), writes=(B["xt"],))
                x1_, x2_ = qtok[:, :, 0:64], qtok[:, :, 64:128]
                o1_, o2_ = rview[:, :, 0:64], rview[:, :, 64:128]
                tA = wsF[:, 0, :].rearrange("p (a b) -> p a b", a=NPT)[:, :, 0:64]
                tB = wsF[:, 1, :].rearrange("p (a b) -> p a b", a=NPT)[:, :, 0:64]
                vop(lambda E, S: E.tensor_tensor(out=tA, in0=x1_, in1=chT[:, :, :], op=ALU.mult),
                    reads=(B["xt"], B["planes"]), writes=(WSFB[0],))
                vop(lambda E, S: E.tensor_tensor(out=tB, in0=x2_, in1=shT[:, :, :], op=ALU.mult),
                    reads=(B["xt"], B["planes"]), writes=(WSFB[1],))
                vop(lambda E, S: E.tensor_sub(out=tA, in0=tA, in1=tB),
                    reads=(WSFB[1],), writes=(WSFB[0],))
                vop(lambda E, S: E.tensor_tensor(out=tB, in0=x1_, in1=clT[:, :, :], op=ALU.mult),
                    reads=(B["xt"], B["planes"]), writes=(WSFB[1],))
                vop(lambda E, S: E.tensor_add(out=tA, in0=tA, in1=tB),
                    reads=(WSFB[1],), writes=(WSFB[0],))
                vop(lambda E, S: E.tensor_tensor(out=tB, in0=x2_, in1=slT[:, :, :], op=ALU.mult),
                    reads=(B["xt"], B["planes"]), writes=(WSFB[1],))
                vop(lambda E, S: E.tensor_sub(out=tA, in0=tA, in1=tB),
                    reads=(WSFB[1],), writes=(WSFB[0],))
                vop(lambda E, S: E.tensor_copy(out=o1_, in_=tA),
                    reads=(WSFB[0],), writes=(B["xnt"],))
                vop(lambda E, S: E.tensor_tensor(out=tA, in0=x1_, in1=shT[:, :, :], op=ALU.mult),
                    reads=(B["xt"], B["planes"]), writes=(WSFB[0],))
                vop(lambda E, S: E.tensor_tensor(out=tB, in0=x2_, in1=chT[:, :, :], op=ALU.mult),
                    reads=(B["xt"], B["planes"]), writes=(WSFB[1],))
                vop(lambda E, S: E.tensor_add(out=tA, in0=tA, in1=tB),
                    reads=(WSFB[1],), writes=(WSFB[0],))
                vop(lambda E, S: E.tensor_tensor(out=tB, in0=x1_, in1=slT[:, :, :], op=ALU.mult),
                    reads=(B["xt"], B["planes"]), writes=(WSFB[1],))
                vop(lambda E, S: E.tensor_add(out=tA, in0=tA, in1=tB),
                    reads=(WSFB[1],), writes=(WSFB[0],))
                vop(lambda E, S: E.tensor_tensor(out=tB, in0=x2_, in1=clT[:, :, :], op=ALU.mult),
                    reads=(B["xt"], B["planes"]), writes=(WSFB[1],))
                vop(lambda E, S: E.tensor_add(out=tA, in0=tA, in1=tB),
                    reads=(WSFB[1],), writes=(WSFB[0],))
                vop(lambda E, S: E.tensor_copy(out=o2_, in_=tA),
                    reads=(WSFB[0],), writes=(B["xnt"],))
                for pt in range(NPT):
                    mm(lambda E, S, pt=pt: E.transpose(
                        ptr2[:, pt % 2, 0:128], xnt[:, pt * 128:(pt + 1) * 128], idb[:, :]),
                       reads=(B["xnt"], B["idb"]), writes=(PTR2B[pt % 2],), fin=True)
                    vop(lambda E, S, pt=pt, dstT=dstT: E.tensor_copy(
                        out=dstT[:, pt * 128:(pt + 1) * 128], in_=ptr2[:, pt % 2, 0:128]),
                        reads=(PTR2B[pt % 2],), writes=(dstB,))

            for kvh in range(L1KV):
                l1_head(wk1, bk1, kvh, knp[:, :], B["knp"], kTb, B["kTb"])
                vv = stream_w(wv1, kvh, 1)
                dmaG(brow[0:1, 128:256], bv1[0:1, kvh * 128:(kvh + 1) * 128],
                     writes=(B["brow"],))
                for pt in range(NPT):
                    bank = pt % 4
                    for z in range(NDT):
                        mm(lambda E, S, z=z, pt=pt, bank=bank, vv=vv: E.matmul(
                            pp[:, bank, 0:128], xnT[:, z, pt * 128:(pt + 1) * 128],
                            vv[:, z, :], start=(z == 0), stop=False),
                           reads=(WSBB[1], B["xnT"]), writes=(PPB[bank],))
                    mm(lambda E, S, bank=bank: E.matmul(
                        pp[:, bank, 0:128], onesb[0:1, :], brow[0:1, 128:256],
                        start=False, stop=True),
                       reads=(B["onesb"], B["brow"]), writes=(PPB[bank],), fin=True)
                    vop(lambda E, S, pt=pt, bank=bank: E.tensor_copy(
                        out=vb[:, pt, :], in_=pp[:, bank, 0:128]),
                        reads=(PPB[bank],), writes=(B["vb"],))
                for qq in range(4):
                    qh = kvh * 4 + qq
                    l1_head(wq1, bq1, qh, qnp[:, :], B["qnp"], qTb, B["qTb"])
                    def sc_part1(i, qh=qh):
                        par = i % 2
                        j = i % 4
                        nk = 128 if i == 0 else 256
                        if i == 0:
                            mm(lambda E, S: E.matmul(psc[:, 0, 0:128], qTb[:, 0:128],
                                                     kTb[:, 0:128], start=True, stop=True),
                               reads=(B["qTb"], B["kTb"]), writes=(PSCB[0],), fin=True)
                        else:
                            mm(lambda E, S, i=i, par=par: E.matmul(
                                psc[:, par, 0:256], qTb[:, i * 128:(i + 1) * 128],
                                kTb[:, (i - 1) * 128:(i + 1) * 128], start=True, stop=True),
                               reads=(B["qTb"], B["kTb"]), writes=(PSCB[par],), fin=True)
                        mrow = masks[:, 0, :] if i > 0 else masks[:, 1, 0:128]
                        vop(lambda E, S, par=par, nk=nk, mrow=mrow: E.scalar_tensor_tensor(
                            out=psc[:, par, 0:nk], in0=psc[:, par, 0:nk],
                            scalar=statB[:, 8:9], in1=mrow, op0=ALU.mult, op1=ALU.add),
                            reads=(B["masks"], B["statB"]), writes=(PSCB[par],))
                        aop(lambda E, S, par=par, j=j, nk=nk: E.activation(
                            out=pbb[:, j, 0:nk], in_=psc[:, par, 0:nk], func=AF.Exp,
                            accum_out=stat[:, 7 + j:8 + j]),
                            reads=(PSCB[par],), writes=(PBBB[j], SMB[j]))
                        vop(lambda E, S, j=j: E.reciprocal(
                            out=stat[:, 24 + j:25 + j], in_=stat[:, 7 + j:8 + j]),
                            reads=(SMB[j],), writes=(SMB[j],))
                        vop(lambda E, S, j=j, nk=nk: E.tensor_scalar(
                            out=pbb[:, j, 0:nk], in0=pbb[:, j, 0:nk],
                            scalar1=stat[:, 24 + j:25 + j], scalar2=None, op0=ALU.mult),
                            reads=(SMB[j],), writes=(PBBB[j],))

                    def tr_part1(i, qh=qh):
                        j = i % 4
                        nk = 128 if i == 0 else 256
                        nkt = nk // 128
                        for kk in range(nkt):
                            mm(lambda E, S, j=j, kk=kk: E.transpose(
                                ptr[:, kk, 0:128], pbb[:, j, kk * 128:(kk + 1) * 128],
                                idb[:, :]),
                               reads=(PBBB[j], B["idb"]), writes=(PTRB[kk],), fin=True)
                            vop(lambda E, S, j=j, kk=kk: E.tensor_copy(
                                out=pTb[:, j, kk, :], in_=ptr[:, kk, 0:128]),
                                reads=(PTRB[kk],), writes=(PTBB[j],))

                    def mv_part1(i, qh=qh):
                        par = i % 2
                        j = i % 4
                        nk = 128 if i == 0 else 256
                        nkt = nk // 128
                        for kk in range(nkt):
                            kpt = (i - 1 + kk) if i > 0 else 0
                            mm(lambda E, S, kpt=kpt, par=par, j=j, kk=kk,
                               st=(kk == 0), sp=(kk == nkt - 1): E.matmul(
                                pav[:, par, 0:128], vb[:, kpt, :], pTb[:, j, kk, :],
                                start=st, stop=sp),
                               reads=(B["vb"], PTBB[j]), writes=(PAVB[par],),
                               fin=(kk == nkt - 1))
                        vop(lambda E, S, par=par, qh=qh, i=i: E.tensor_copy(
                            out=avT[:, qh, i * 128:(i + 1) * 128], in_=pav[:, par, 0:128]),
                            reads=(PAVB[par],), writes=(B_arA[qh],))

                    sc_part1(0)
                    sc_part1(1)
                    tr_part1(0)
                    for i in range(NB):
                        if i + 2 < NB:
                            sc_part1(i + 2)
                        if i + 1 < NB:
                            tr_part1(i + 1)
                        mv_part1(i)

            y_project(wo1, x1d, xpd, B["xpd"], bo1, None)

            def ag1_w(ws, ft):
                sh, lt = ft // F1TPC, ft % F1TPC
                off = sh * L1_CON + ws * L1_WSZ + lt * (D * 128)
                return ag1o[off:off + D * 128].rearrange("(p f) -> p f", p=128)

            def ag1_rhs(ft, dc):
                sh, lt = ft // F1TPC, ft % F1TPC
                off = sh * L1_CON + 2 * L1_WSZ + lt * (128 * D)
                fw = 128 if ft < 42 else F1 - 42 * 128
                return 1, [(wsC[0:fw, ft % 4, :],
                            ag1o[off:off + 128 * D].rearrange("(p f) -> p f", p=128)
                            [0:fw, dc * 512:(dc + 1) * 512])]

            _ck(10)
            ffn(ag1_w, ag1_rhs, NFT1, xpd, out, B["x1d"], statB[:, 3:4], statB[:, 6:7],
                agtok["ag1"])

        except _SB:
            pass

        for j in range(KQ):
            if p.cnt.get("qG%d" % j):
                p._wait("gpsimd", ("qG%d" % j, p.cnt["qG%d" % j]))
            if p.cnt.get("qS%d" % j):
                p._wait("sync", ("qS%d" % j, p.cnt["qS%d" % j]))

        import os
        for _e in ENGS:
            _tv = int(os.environ.get("TRUNC_" + _e, "0") or "0")
            if _tv:
                p.ops[_e] = p.ops[_e][:_tv]
        if os.environ.get("PRINTNV"):
            print("NVOPS", {e: len(p.ops[e]) for e in ENGS}, flush=True)

        with nc.Block() as block:
            @block.sync
            def _(E):
                for fn in p.ops["sync"]:
                    fn(E, sems)

            @block.scalar
            def _(E):
                for fn in p.ops["scalar"]:
                    fn(E, sems)

            @block.vector
            def _(E):
                for fn in p.ops["vector"]:
                    fn(E, sems)

            @block.tensor
            def _(E):
                for fn in p.ops["tensor"]:
                    fn(E, sems)

            @block.gpsimd
            def _(E):
                for fn in p.ops["gpsimd"]:
                    fn(E, sems)

    return nc


def kernel(L0_attn_x, L0_attn_Wq, L0_attn_Wk, L0_attn_Wv, L0_attn_Wo,
           L0_ffn_weight1, L0_ffn_weight2, L0_ffn_weight3,
           L1_attn_Wq, L1_attn_bq, L1_attn_Wk, L1_attn_bk, L1_attn_Wv, L1_attn_bv,
           L1_attn_Wo, L1_attn_bo, L1_attn_qn_w, L1_attn_kn_w, L1_attn_scale,
           L1_ffn_w1, L1_ffn_w3, L1_ffn_w2):
    perm = np.arange(128).reshape(64, 2).T.reshape(-1)
    permq = (np.arange(L1H)[:, None] * 128 + perm[None, :]).reshape(-1)
    permk = (np.arange(L1KV)[:, None] * 128 + perm[None, :]).reshape(-1)

    cT = lambda a: np.ascontiguousarray(np.asarray(a).T)
    w1p = np.zeros((8 * F0P, D), dtype=BFD)
    w1p[:F0] = np.asarray(L0_ffn_weight1)
    w2p = np.zeros((8 * F0P, D), dtype=BFD)
    w2p[:F0] = np.asarray(L0_ffn_weight2)
    w3f = np.asarray(L0_ffn_weight3)
    w1T1 = np.zeros((D, 8 * F1P), dtype=np.float32)
    w1T1[:, :F1] = np.asarray(L1_ffn_w1).T
    w3T1 = np.zeros((D, 8 * F1P), dtype=np.float32)
    w3T1[:, :F1] = np.asarray(L1_ffn_w3).T
    w2T1 = np.zeros((8 * F1P, D), dtype=np.float32)
    w2T1[:F1] = np.asarray(L1_ffn_w2).T

    def tile_cols(arrT):
        Din, ncols = arrT.shape
        nhg = ncols // 128
        o = np.empty((nhg, 128, Din), dtype=arrT.dtype)
        for hg in range(nhg):
            o[hg] = (arrT[:, hg * 128:(hg + 1) * 128]
                     .reshape(Din // 128, 128, 128).transpose(1, 0, 2)
                     .reshape(128, Din))
        return np.ascontiguousarray(o)

    shared = {
        "wq0": tile_cols(cT(L0_attn_Wq)), "wk0": tile_cols(cT(L0_attn_Wk)),
        "wv0": tile_cols(cT(L0_attn_Wv)),
        "wo0": cT(L0_attn_Wo),
        "wq1": tile_cols(np.ascontiguousarray(np.asarray(L1_attn_Wq)[permq].T)),
        "wk1": tile_cols(np.ascontiguousarray(np.asarray(L1_attn_Wk)[permk].T)),
        "wv1": tile_cols(cT(L1_attn_Wv)), "wo1": cT(L1_attn_Wo),
        "bq1": np.ascontiguousarray(np.asarray(L1_attn_bq)[permq].reshape(1, D)),
        "bk1": np.ascontiguousarray(np.asarray(L1_attn_bk)[permk].reshape(1, 512)),
        "bv1": np.ascontiguousarray(np.asarray(L1_attn_bv).reshape(1, 512)),
        "bo1": np.ascontiguousarray(np.asarray(L1_attn_bo).reshape(1, D)),
        "qn1": np.ascontiguousarray(np.asarray(L1_attn_qn_w)[perm].reshape(1, 128)),
        "kn1": np.ascontiguousarray(np.asarray(L1_attn_kn_w)[perm].reshape(1, 128)),
        "sc1": np.asarray(L1_attn_scale).reshape(1, 1).astype(np.float32),
    }
    in_maps = []
    for c in range(NCORE):
        m = dict(shared)
        m["x"] = np.ascontiguousarray(np.asarray(L0_attn_x)[c])
        m["w1s0"] = np.ascontiguousarray(w1p[c * F0P:(c + 1) * F0P])
        m["w2s0"] = np.ascontiguousarray(w2p[c * F0P:(c + 1) * F0P])
        m["w3s0"] = np.ascontiguousarray(w3f[c * (D // 8):(c + 1) * (D // 8)])
        m["w1s1"] = np.ascontiguousarray(w1T1[:, c * F1P:(c + 1) * F1P])
        m["w3s1"] = np.ascontiguousarray(w3T1[:, c * F1P:(c + 1) * F1P])
        m["w2s1"] = np.ascontiguousarray(w2T1[c * F1P:(c + 1) * F1P])
        in_maps.append(m)

    nc = build_nc()
    res = run_bass_kernel_spmd(nc, in_maps, core_ids=list(range(NCORE)))
    return np.stack([res.results[c]["out"] for c in range(NCORE)], axis=0)


# version 15
# speedup vs baseline: 1.1722x; 1.1722x over previous
import numpy as np
import ml_dtypes
import concourse.bass as bass
import concourse.mybir as mybir
from concourse.bass_utils import run_bass_kernel_spmd

BFD = ml_dtypes.bfloat16
F32 = mybir.dt.float32
BF16 = mybir.dt.bfloat16
AF = mybir.ActivationFunctionType
ALU = mybir.AluOpType
AX = mybir.AxisListType

D = 2048
T = 2048
NCORE = 8
NPT = T // 128
NDT = D // 128

L0H = 32
NB = 16
F0 = 5632
F0TPC = 6
F0P = F0TPC * 128
NF4 = np.array([-1.0, -0.6962890625, -0.52490234375, -0.39453125, -0.284423828125,
                -0.18359375, -0.091064453125, 0.0, 0.07958984375, 0.160888671875,
                0.24609375, 0.337890625, 0.440673828125, 0.5625, 0.72265625, 1.0],
               dtype=np.float64)
NF4_MID = [float(np.float32((NF4[i] + NF4[i + 1]) / 2)) for i in range(15)]
NF4_STEP = [float(np.float32(NF4[i + 1] - NF4[i])) for i in range(15)]
NF4_0 = float(np.float32(NF4[0]))
NBLK_W = 180224.0
L1H = 16
L1KV = 4
RMS_EPS = 1e-8
ROPE_BASE = 10000.0
MASK0 = -3.3895313892515355e38
MASK1 = -10000.0
F1 = 5461
F1TPC = 6
F1P = F1TPC * 128
NELT_T = float(F1 * D)
NFT1 = 43
MAGIC = 12582912.0

ENGS = ("sync", "scalar", "vector", "tensor", "gpsimd")


class Buf:
    __slots__ = ("w", "rs")

    def __init__(self):
        self.w = None
        self.rs = []

    def for_read(self):
        return (self.w,) if self.w else ()

    def for_write(self):
        return tuple(self.rs) + ((self.w,) if self.w else ())

    def add_read(self, tok):
        if tok is not None:
            self.rs.append(tok)

    def set_write(self, tok):
        self.w = tok
        self.rs = []


class P:
    def __init__(self):
        self.ops = {e: [] for e in ENGS}
        self.cnt = {}
        self.waited = {e: {} for e in ENGS}
        self.tags = {e: [] for e in ENGS}

    def _tag(self, eng, kind):
        import traceback
        fs = traceback.extract_stack(limit=6)
        lines = [f.lineno for f in fs if f.filename.endswith("kernel.py")]
        self.tags[eng].append((kind, lines))

    def _wait(self, eng, tok):
        sem, val = tok
        if self.waited[eng].get(sem, 0) >= val:
            return
        self.waited[eng][sem] = val

        def fn(E, S, sem=sem, val=val):
            E.wait_ge(S[sem], val)
        self._tag(eng, f"wait {sem}>={val}")
        self.ops[eng].append(fn)

    def do(self, eng, build, reads=(), writes=(), waits=(), inc=None, inck=1,
           serialize=False):
        seen = []
        if serialize and inc is not None:
            pv = self.cnt.get(inc, 0)
            if pv:
                seen.append((inc, pv))
        for t in waits:
            if t is not None:
                seen.append(t)
        for b in reads:
            seen.extend(b.for_read())
        for b in writes:
            seen.extend(b.for_write())
        for t in seen:
            self._wait(eng, t)
        tok = None
        if inc is not None:
            self.cnt[inc] = self.cnt.get(inc, 0) + inck
            tok = (inc, self.cnt[inc])

        def fn(E, S, build=build, inc=inc, inck=inck):
            inst = build(E, S)
            if inc is not None:
                inst.then_inc(S[inc], inck)
        self._tag(eng, "op")
        self.ops[eng].append(fn)
        for b in reads:
            b.add_read(tok)
        for b in writes:
            b.set_write(tok)
        return tok


def build_nc():
    nc = bass.Bass("TRN2", target_bir_lowering=False)
    p = P()

    def din(name, shape, dt):
        return nc.dram_tensor(name, list(shape), dt, kind="ExternalInput").ap()

    x_in = din("x", (T, D), F32)
    wq0 = din("wq0", (16, 128, D), F32)
    wk0 = din("wk0", (16, 128, D), F32)
    wv0 = din("wv0", (16, 128, D), F32)
    wo0 = din("wo0", (D, D), F32)
    w1s0 = din("w1s0", (F0P, D), BF16)
    w2s0 = din("w2s0", (F0P, D), BF16)
    w3s0 = din("w3s0", (D // 8, F0), BF16)
    wq1 = din("wq1", (16, 128, D), F32)
    wk1 = din("wk1", (4, 128, D), F32)
    wv1 = din("wv1", (4, 128, D), F32)
    wo1 = din("wo1", (D, D), F32)
    bq1 = din("bq1", (1, D), F32)
    bk1 = din("bk1", (1, 512), F32)
    bv1 = din("bv1", (1, 512), F32)
    bo1 = din("bo1", (1, D), F32)
    qn1 = din("qn1", (1, 128), F32)
    kn1 = din("kn1", (1, 128), F32)
    sc1 = din("sc1", (1, 1), F32)
    w1s1 = din("w1s1", (D, F1P), F32)
    w3s1 = din("w3s1", (D, F1P), F32)
    w2s1 = din("w2s1", (F1P, D), F32)

    out = nc.dram_tensor("out", [T, D], F32, kind="ExternalOutput").ap()

    x1d = nc.dram_tensor("x1d", [T, D], F32).ap()
    wo0b = nc.dram_tensor("wo0b", [D, D], BF16).ap()
    wo1b = nc.dram_tensor("wo1b", [D, D], BF16).ap()
    xpd = nc.dram_tensor("xpd", [T, D], F32).ap()
    NS1 = F0P * 32
    NS3 = (D // 8) * 88
    scf = nc.dram_tensor("scf", [2 * NS1 + NS3], F32).ap()
    scb = nc.dram_tensor("scb", [2 * NS1 + NS3], BF16).ap()
    L0_W1SZ = F0TPC * D * 128
    L0_W3SZ = 44 * 128 * 256
    L0_CON = 2 * L0_W1SZ + L0_W3SZ
    ag0i = nc.dram_tensor("ag0i", [L0_CON], BF16).ap()
    ag0o = nc.dram_tensor("ag0o", [8 * L0_CON], BF16, addr_space="Shared").ap()
    L1_WSZ = F1TPC * D * 128
    L1_CON = 3 * L1_WSZ
    ag1i = nc.dram_tensor("ag1i", [L1_CON], BF16).ap()
    ag1o = nc.dram_tensor("ag1o", [8 * L1_CON], BF16, addr_space="Shared").ap()
    ari = nc.dram_tensor("ari", [1, 6], F32).ap()
    aro = nc.dram_tensor("aro", [1, 6], F32).ap()

    ident = nc.inline_tensor(np.eye(128, dtype=BFD), name="identb").ap()
    r = np.arange(128)[:, None]
    c256 = np.arange(256)[None, :]
    c128 = np.arange(128)[None, :]
    m0a_t = nc.inline_tensor(
        np.where((c256 >= r) & (c256 <= r + 128), 0.0, MASK0).astype(BFD), name="m0a").ap()
    m0b_t = nc.inline_tensor(
        np.where(c128 <= r, 0.0, MASK0).astype(BFD), name="m0b").ap()
    m1a_t = nc.inline_tensor(
        np.where((c256 >= r + 64) & (c256 <= r + 128), 0.0, MASK1).astype(BFD), name="m1a").ap()
    m1b_t = nc.inline_tensor(
        np.where((c128 <= r) & (c128 >= r - 64), 0.0, MASK1).astype(BFD), name="m1b").ap()
    inv32 = (1.0 / (ROPE_BASE ** (np.arange(0, 128, 2, dtype=np.float64) / 128.0))
             ).astype(np.float32)
    fr32 = np.arange(T, dtype=np.float32)[:, None] * inv32[None, :]
    ch32 = np.cos(fr32).reshape(NPT, 128, 64).transpose(1, 0, 2).reshape(128, NPT * 64)
    sh32 = np.sin(fr32).reshape(NPT, 128, 64).transpose(1, 0, 2).reshape(128, NPT * 64)
    cosh_t = nc.inline_tensor(np.ascontiguousarray(ch32), name="cosh").ap()
    sinh_t = nc.inline_tensor(np.ascontiguousarray(sh32), name="sinh").ap()
    ones_t = nc.inline_tensor(np.ones((1, 128), dtype=BFD), name="onesr").ap()
    onescol_t = nc.inline_tensor(np.ones((128, 1), dtype=np.float32), name="onesc").ap()

    import contextlib
    stack = contextlib.ExitStack()

    def sb(name, shape, dt):
        return stack.enter_context(nc.sbuf_tensor(name, list(shape), dt))

    with stack:
        xnT = sb("xnT", (128, NDT, T), BF16)
        arA = sb("arA", (128, 16, T), BF16)
        wsF = sb("wsF", (128, 2, 1024), F32)
        wsB = sb("wsB", (128, 3, 2048), BF16)
        wsC = sb("wsC", (128, 4, 512), BF16)
        xt = sb("xt", (128, 2048), F32)
        xt2 = sb("xt2", (128, 2, 512), F32)
        xnt = sb("xnt", (128, 2048), BF16)
        qTb = sb("qTb", (128, 2048), BF16)
        kTb = sb("kTb", (128, 2048), BF16)
        vb = sb("vb", (128, 16, 128), BF16)
        chT = sb("chT", (128, NPT, 64), F32)
        shT = sb("shT", (128, NPT, 64), F32)
        pbb = sb("pbb", (128, 4, 256), BF16)
        pTb = sb("pTb", (128, 4, 2, 128), BF16)
        stat = sb("stat", (128, 28), F32)
        statB = sb("statB", (128, 16), F32)
        s1b = sb("s1b", (128, 560), F32)
        s2b = sb("s2b", (128, 560), BF16)
        sqw = sb("sqw", (128, 256), F32)
        sqw2 = sb("sqw2", (128, 256), F32)
        dsc = arA[:, 13, :]
        dhh = arA[:, 14, :]
        dac = arA[:, 15, :]
        masks = sb("masks", (128, 2, 256), BF16)
        idb = sb("idb", (128, 128), BF16)
        onesb = sb("onesb", (1, 128), BF16)
        onesf = sb("onesf", (1, 128), F32)
        onescol = sb("onescol", (128, 1), F32)
        brow = sb("brow", (1, 512), BF16)
        frow = sb("frow", (1, 128), F32)
        qnp = sb("qnp", (128, 128), F32)
        knp = sb("knp", (128, 128), F32)
        arow = sb("arow", (1, 16), F32)
        cst = sb("cst", (128, 4), F32)

        pp = stack.enter_context(nc.psum_tensor("pp", [128, 4, 512], F32))
        ptr = stack.enter_context(nc.psum_tensor("ptr", [128, 2, 1024], BF16))
        ptr2 = stack.enter_context(nc.psum_tensor("ptr2", [128, 2, 1024], BF16))
        psc = pp
        pav = pp[:, 2:4, :]

        KQ = 12
        sems = {}
        for s in ("cc", "tp", "vp", "ap", "gp"):
            sems[s] = stack.enter_context(nc.semaphore(s))
        for j in range(KQ):
            sems["qS%d" % j] = stack.enter_context(nc.semaphore("qS%d" % j))
            sems["qG%d" % j] = stack.enter_context(nc.semaphore("qG%d" % j))
        qn = {"S": 0, "G": 0}

        B = {}
        for name in ("xnT", "arA", "wsF0", "wsF1", "wsB0", "wsB1", "wsB2", "wsC0", "wsC1",
                     "xt", "xt20", "xt21", "xnt", "qTb", "kTb", "vb", "planes", "wsC2", "wsC3",
                     "pbb0", "pbb1", "pbb2", "pbb3", "pT0", "pT1", "pT2", "pT3", "statLN",
                     "statSM0", "statSM1", "statSM2", "statSM3", "ptr20", "ptr21",
                     "statRMS", "statNF", "statTHR", "statACC", "statB", "s1b", "s2b",
                     "sqw", "sqw2", "dsc", "dhh", "dac", "masks", "idb", "onesb", "onesf",
                     "onescol", "brow", "frow", "qnp", "knp", "arow", "cst",
                     "pp0", "pp1", "pp2", "pp3", "psc0", "psc1", "ptr0", "ptr1",
                     "pav0", "pav1", "xpd", "x1d", "scf", "scb", "ag0i", "ag1i",
                     "ari", "aro", "wo0b", "wo1b"):
            B[name] = Buf()
        PPB = [B["pp0"], B["pp1"], B["pp2"], B["pp3"]]
        PSCB = [B["pp0"], B["pp1"]]
        PTRB = [B["ptr0"], B["ptr1"]]
        PTR2B = [B["ptr20"], B["ptr21"]]
        PAVB = [B["pp2"], B["pp3"]]
        WSFB = [B["wsF0"], B["wsF1"]]
        WSBB = [B["wsB0"], B["wsB1"], B["wsB2"]]
        WSCB = [B["wsC0"], B["wsC1"], B["wsC2"], B["wsC3"]]
        XT2B = [B["xt20"], B["xt21"]]
        B_arA = [Buf() for _ in range(16)]
        PBBB = [B["pbb0"], B["pbb1"], B["pbb2"], B["pbb3"]]
        PTBB = [B["pT0"], B["pT1"], B["pT2"], B["pT3"]]
        SMB = [B["statSM0"], B["statSM1"], B["statSM2"], B["statSM3"]]

        def dmaS(dst, src, reads=(), writes=(), waits=()):
            sem = "qS%d" % (qn["S"] % KQ)
            qn["S"] += 1
            return p.do("sync", lambda E, S, d=dst, s=src: E.dma_start(out=d, in_=s),
                        reads=reads, writes=writes, waits=waits, inc=sem, inck=16,
                        serialize=True)

        def dmaG(dst, src, reads=(), writes=(), waits=()):
            sem = "qG%d" % (qn["G"] % KQ)
            qn["G"] += 1
            return p.do("gpsimd", lambda E, S, d=dst, s=src: E.dma_start(out=d, in_=s),
                        reads=reads, writes=writes, waits=waits, inc=sem, inck=16,
                        serialize=True)

        def vop(build, reads=(), writes=(), waits=()):
            return p.do("vector", build, reads=reads, writes=writes, waits=waits, inc="vp")

        def aop(build, reads=(), writes=(), waits=()):
            return p.do("scalar", build, reads=reads, writes=writes, waits=waits, inc="ap")

        def mm(build, reads=(), writes=(), waits=(), fin=False):
            return p.do("tensor", build, reads=reads, writes=writes, waits=waits,
                        inc="tp" if fin else None)

        class _SB(Exception):
            pass

        import os as _os
        _PH = int(_os.environ.get("PHASE", "99") or "99")

        def _ck(n):
            if _PH < n:
                raise _SB()
        try:
            vop(lambda E, S: E.memset(cst[:, 0:1], 1e-5), writes=(B["cst"],))
            vop(lambda E, S: E.memset(cst[:, 1:2], RMS_EPS), writes=(B["cst"],))
            vop(lambda E, S: E.memset(cst[:, 2:3], 1.0 / 128.0), writes=(B["cst"],))
            vop(lambda E, S: E.memset(onesf[0:1, :], 1.0), writes=(B["onesf"],))
            dmaS(idb[:, :], ident[:, :], writes=(B["idb"],))
            dmaS(masks[:, 0, :], m0a_t[:, :], writes=(B["masks"],))
            dmaS(masks[:, 1, 0:128], m0b_t[:, :], writes=(B["masks"],))
            dmaS(onesb[:, :], ones_t[:, :], writes=(B["onesb"],))
            dmaS(onescol[:, :], onescol_t[:, :], writes=(B["onescol"],))

            vop(lambda E, S: E.memset(stat[:, 0:8], 0.0), writes=(B["statACC"],))

            nf4_tiles = []
            for ti in range(F0TPC):
                nf4_tiles.append((0, w1s0[ti * 128:(ti + 1) * 128, :], ti * 32, 32))
            for ti in range(F0TPC):
                nf4_tiles.append((1, w2s0[ti * 128:(ti + 1) * 128, :], 192 + ti * 32, 32))
            for ptd in range(2):
                for ci, (c0, cw) in enumerate(((0, 2048), (2048, 2048), (4096, 1536))):
                    nf4_tiles.append((2, w3s0[ptd * 128:(ptd + 1) * 128, c0:c0 + cw],
                                      384 + ptd * 88 + (0, 32, 64)[ci], cw // 64))

            for col, src_ap, s1off, nb in nf4_tiles:
                tld = dmaS(dhh[:, 0:nb * 64], src_ap, writes=(B["dhh"],))
                vop(lambda E, S, nb=nb, o=s1off: E.reduce_max(
                    out=s1b[:, o:o + nb],
                    in_=dhh[:, 0:nb * 64].rearrange("p (a b) -> p a b", a=nb),
                    axis=AX.X, apply_absolute_value=True),
                    reads=(B["dhh"],), writes=(B["s1b"],))
                vop(lambda E, S, nb=nb, o=s1off: E.reduce_sum(
                    out=stat[:, 20:21], in_=s1b[:, o:o + nb], axis=AX.X),
                    reads=(B["s1b"],), writes=(B["statNF"],))
                vop(lambda E, S, col=col: E.tensor_add(
                    out=stat[:, col:col + 1], in0=stat[:, col:col + 1], in1=stat[:, 20:21]),
                    reads=(B["statNF"],), writes=(B["statACC"],))

            for col, src, nr, ncc in ((3, w1s1, D, F1P), (4, w3s1, D, F1P), (5, w2s1, F1P, D)):
                for ti in range(nr // 128):
                    tld = dmaS(xt[:, 0:ncc], src[ti * 128:(ti + 1) * 128, :], writes=(B["xt"],))
                    vop(lambda E, S, ncc=ncc: E.reduce_sum(
                        out=stat[:, 8:8 + ncc // 128],
                        in_=xt[:, 0:ncc].rearrange("p (a b) -> p a b", b=128),
                        axis=AX.X, apply_absolute_value=True),
                        reads=(B["xt"],), writes=(B["statNF"],))
                    vop(lambda E, S, ncc=ncc: E.reduce_sum(
                        out=stat[:, 20:21], in_=stat[:, 8:8 + ncc // 128], axis=AX.X),
                        reads=(B["statNF"],), writes=(B["statNF"],))
                    vop(lambda E, S, col=col: E.tensor_add(
                        out=stat[:, col:col + 1], in0=stat[:, col:col + 1], in1=stat[:, 20:21]),
                        reads=(B["statNF"],), writes=(B["statACC"],))

            mm(lambda E, S: E.matmul(pav[0:6, 0, 0:1], stat[:, 0:6], onescol[:, :],
                                     start=True, stop=True),
               reads=(B["statACC"], B["onescol"]), writes=(PAVB[0],), fin=True)
            vop(lambda E, S: E.tensor_copy(out=stat[0:6, 21:22], in_=pav[0:6, 0, 0:1]),
                reads=(PAVB[0],), writes=(B["statNF"],))
            tok_ari = dmaG(ari[0:1, 0:6].rearrange("a b -> (a b)").rearrange("(b c) -> b c", c=1),
                           stat[0:6, 21:22], reads=(B["statNF"],), writes=(B["ari"],))
            tok_cc1 = p.do("gpsimd",
                           lambda E, S: E.collective_compute(
                               "AllReduce", ALU.add, replica_groups=[list(range(8))],
                               ins=[ari.opt()], outs=[aro.opt()]),
                           reads=(B["ari"],), writes=(B["aro"],), inc="cc", serialize=True)
            dmaG(arow[0:1, 0:6], aro[0:1, 0:6], reads=(B["aro"],), writes=(B["arow"],))

            for i, (dst, srcc, mul) in enumerate(
                    [(8, 0, 1.0 / NBLK_W), (9, 1, 1.0 / NBLK_W), (10, 2, 1.0 / NBLK_W),
                     (11, 3, 1.0 / NELT_T), (12, 4, 1.0 / NELT_T), (13, 5, 1.0 / NELT_T)]):
                vop(lambda E, S, dst=dst, srcc=srcc, mul=mul: E.tensor_scalar(
                    out=arow[0:1, dst:dst + 1], in0=arow[0:1, srcc:srcc + 1],
                    scalar1=mul, scalar2=None, op0=ALU.mult),
                    reads=(B["arow"],), writes=(B["arow"],))
            vop(lambda E, S: E.tensor_mul(out=arow[0:1, 14:15], in0=arow[0:1, 12:13],
                                          in1=arow[0:1, 13:14]),
                reads=(B["arow"],), writes=(B["arow"],))
            mm(lambda E, S: E.matmul(pav[:, 0, 0:7], onesf[0:1, :], arow[0:1, 8:15],
                                     start=True, stop=True),
               reads=(B["arow"], B["onesf"]), writes=(PAVB[0],), fin=True)
            vop(lambda E, S: E.tensor_copy(out=statB[:, 0:7], in_=pav[:, 0, 0:7]),
                reads=(PAVB[0],), writes=(B["statB"],))

            def scaler_quant(mcol, s1off, ntile, nbt, nsc, doff, layout3=False):
                if not layout3:
                    dmaG(scf[doff:doff + nsc].rearrange("(t p b) -> p t b", t=ntile, p=128),
                         s1b[:, s1off:s1off + ntile * nbt].rearrange("p (t b) -> p t b", t=ntile),
                         reads=(B["s1b"],), writes=(B["scf"],))
                else:
                    dmaG(scf[doff:doff + nsc].rearrange("(t p b) -> p t b", t=2, p=128),
                         s1b[:, s1off:s1off + 176].rearrange("p (t b) -> p t b", t=2),
                         reads=(B["s1b"],), writes=(B["scf"],))
                nsb = nsc // 256
                dmaS(sqw[0:nsb, :], scf[doff:doff + nsc].rearrange("(q j) -> q j", q=nsb),
                     reads=(B["scf"],), writes=(B["sqw"],))
                vop(lambda E, S, nsb=nsb, mcol=mcol: E.tensor_scalar(
                    out=sqw[0:nsb, :], in0=sqw[0:nsb, :], scalar1=statB[0:nsb, mcol:mcol + 1],
                    scalar2=None, op0=ALU.subtract),
                    reads=(B["statB"],), writes=(B["sqw"],))
                vop(lambda E, S, nsb=nsb: E.reduce_max(
                    out=stat[0:nsb, 17:18], in_=sqw[0:nsb, :], axis=AX.X,
                    apply_absolute_value=True), reads=(B["sqw"],), writes=(B["statNF"],))
                vop(lambda E, S, nsb=nsb: E.reciprocal(out=stat[0:nsb, 18:19],
                                                       in_=stat[0:nsb, 17:18]),
                    reads=(B["statNF"],), writes=(B["statNF"],))
                vop(lambda E, S, nsb=nsb: E.tensor_scalar(
                    out=stat[0:nsb, 18:19], in0=stat[0:nsb, 18:19], scalar1=128.0,
                    scalar2=None, op0=ALU.mult), reads=(B["statNF"],), writes=(B["statNF"],))
                vop(lambda E, S, nsb=nsb: E.tensor_scalar(
                    out=stat[0:nsb, 19:20], in0=stat[0:nsb, 17:18], scalar1=1.0 / 128.0,
                    scalar2=None, op0=ALU.mult), reads=(B["statNF"],), writes=(B["statNF"],))
                vop(lambda E, S, nsb=nsb: E.tensor_scalar(
                    out=sqw2[0:nsb, :], in0=sqw[0:nsb, :], scalar1=stat[0:nsb, 18:19],
                    scalar2=MAGIC, op0=ALU.mult, op1=ALU.add),
                    reads=(B["sqw"], B["statNF"]), writes=(B["sqw2"],))
                vop(lambda E, S, nsb=nsb: E.tensor_scalar(
                    out=sqw2[0:nsb, :], in0=sqw2[0:nsb, :], scalar1=MAGIC,
                    scalar2=None, op0=ALU.subtract),
                    writes=(B["sqw2"],))
                vop(lambda E, S, nsb=nsb: E.tensor_scalar(
                    out=sqw2[0:nsb, :], in0=sqw2[0:nsb, :], scalar1=127.0, scalar2=-128.0,
                    op0=ALU.min, op1=ALU.max), writes=(B["sqw2"],))
                vop(lambda E, S, nsb=nsb: E.tensor_scalar(
                    out=sqw[0:nsb, :], in0=sqw2[0:nsb, :], scalar1=stat[0:nsb, 19:20],
                    scalar2=None, op0=ALU.mult),
                    reads=(B["sqw2"], B["statNF"]), writes=(B["sqw"],))
                vop(lambda E, S, nsb=nsb: E.tensor_copy(out=dsc[0:nsb, 0:256],
                                                        in_=sqw[0:nsb, :]),
                    reads=(B["sqw"],), writes=(B["dsc"],))
                vop(lambda E, S, nsb=nsb, mcol=mcol: E.tensor_scalar(
                    out=dsc[0:nsb, 0:256], in0=dsc[0:nsb, 0:256],
                    scalar1=statB[0:nsb, mcol:mcol + 1], scalar2=None, op0=ALU.add),
                    reads=(B["statB"],), writes=(B["dsc"],))
                dmaG(scb[doff:doff + nsc].rearrange("(q j) -> q j", q=nsb), dsc[0:nsb, 0:256],
                     reads=(B["dsc"],), writes=(B["scb"],))
                if not layout3:
                    dmaS(s2b[:, s1off:s1off + ntile * nbt].rearrange("p (t b) -> p t b", t=ntile),
                         scb[doff:doff + nsc].rearrange("(t p b) -> p t b", t=ntile, p=128),
                         reads=(B["scb"],), writes=(B["s2b"],))
                else:
                    dmaS(s2b[:, s1off:s1off + 176].rearrange("p (t b) -> p t b", t=2),
                         scb[doff:doff + nsc].rearrange("(t p b) -> p t b", t=2, p=128),
                         reads=(B["scb"],), writes=(B["s2b"],))

            p0_units = []
            agtok = {}
            _ck(2)
            p0_units.append(lambda: scaler_quant(0, 0, F0TPC, 32, NS1, 0))
            p0_units.append(lambda: scaler_quant(1, 192, F0TPC, 32, NS1, NS1))
            p0_units.append(lambda: scaler_quant(2, 384, 2, 88, NS3, 2 * NS1,
                                                 layout3=True))

            def nf4_dequant_tile(src_ap, s1off, nb):
                dmaS(dhh[:, 0:nb * 64], src_ap, writes=(B["dhh"],))
                vop(lambda E, S, o=s1off, nb=nb: E.tensor_scalar(
                    out=s1b[:, o:o + nb], in0=s1b[:, o:o + nb], scalar1=1e-30,
                    scalar2=None, op0=ALU.max), writes=(B["s1b"],))
                vop(lambda E, S, o=s1off, nb=nb: E.reciprocal(
                    out=sqw2[:, 0:nb], in_=s1b[:, o:o + nb]),
                    reads=(B["s1b"],), writes=(B["sqw2"],))
                vop(lambda E, S, nb=nb: E.tensor_tensor(
                    out=dsc[:, 0:nb * 64].rearrange("p (a b) -> p a b", a=nb),
                    in0=dhh[:, 0:nb * 64].rearrange("p (a b) -> p a b", a=nb),
                    in1=sqw2[:, 0:nb].rearrange("p (a u) -> p a u", u=1)
                        .broadcast_to([128, nb, 64]),
                    op=ALU.mult), reads=(B["dhh"], B["sqw2"]), writes=(B["dsc"],))
                vop(lambda E, S, nb=nb: E.memset(dac[:, 0:nb * 64], NF4_0), writes=(B["dac"],))
                for j in range(15):
                    vop(lambda E, S, nb=nb, j=j: E.tensor_scalar(
                        out=dhh[:, 0:nb * 64], in0=dsc[:, 0:nb * 64], scalar1=NF4_MID[j],
                        scalar2=None, op0=ALU.is_gt), reads=(B["dsc"],), writes=(B["dhh"],))
                    vop(lambda E, S, nb=nb, j=j: E.scalar_tensor_tensor(
                        out=dac[:, 0:nb * 64], in0=dhh[:, 0:nb * 64], scalar=NF4_STEP[j],
                        in1=dac[:, 0:nb * 64], op0=ALU.mult, op1=ALU.add),
                        reads=(B["dhh"],), writes=(B["dac"],))
                return vop(lambda E, S, o=s1off, nb=nb: E.tensor_tensor(
                    out=dsc[:, 0:nb * 64].rearrange("p (a b) -> p a b", a=nb),
                    in0=dac[:, 0:nb * 64].rearrange("p (a b) -> p a b", a=nb),
                    in1=s2b[:, o:o + nb].rearrange("p (a u) -> p a u", u=1)
                        .broadcast_to([128, nb, 64]),
                    op=ALU.mult), reads=(B["dac"], B["s2b"]), writes=(B["dsc"],))

            def transpose_store(nchunks, dst_fn):
                for j in range(nchunks):
                    mm(lambda E, S, j=j: E.transpose(ptr2[:, j % 2, 0:128],
                                                     dsc[:, j * 128:(j + 1) * 128], idb[:, :]),
                       reads=(B["dsc"], B["idb"]), writes=(PTR2B[j % 2],), fin=True)
                    aop(lambda E, S, j=j: E.copy(
                        out=xnt[:, (j % 2) * 128:(j % 2) * 128 + 128], in_=ptr2[:, j % 2, 0:128]),
                        reads=(PTR2B[j % 2],), writes=(B["xnt"],))
                    dmaG(dst_fn(j), xnt[:, (j % 2) * 128:(j % 2) * 128 + 128],
                         reads=(B["xnt"],), writes=(B["ag0i"],))

            _ck(3)
            def u_nf4(wslot, src, ti):
                cb = wslot * L0_W1SZ
                nf4_dequant_tile(src[ti * 128:(ti + 1) * 128, :], wslot * 192 + ti * 32, 32)
                transpose_store(NDT, lambda j, cb=cb, ti=ti: ag0i[
                    cb + ti * (D * 128):cb + (ti + 1) * (D * 128)]
                    .rearrange("(p a f) -> p a f", p=128, a=NDT)[:, j, :])

            def u_nf4w3(ptd, ci):
                cb = 2 * L0_W1SZ
                c0, cw = ((0, 2048), (2048, 2048), (4096, 1536))[ci]
                nf4_dequant_tile(w3s0[ptd * 128:(ptd + 1) * 128, c0:c0 + cw],
                                 384 + ptd * 88 + (0, 32, 64)[ci], cw // 64)
                transpose_store(cw // 128, lambda j, cb=cb, c0=c0, ptd=ptd: ag0i[
                    cb + ((c0 // 128) + j) * (128 * 256):
                    cb + ((c0 // 128) + j + 1) * (128 * 256)]
                    .rearrange("(p f) -> p f", p=128)[:, ptd * 128:(ptd + 1) * 128])

            for wslot, src in ((0, w1s0), (1, w2s0)):
                for ti in range(F0TPC):
                    p0_units.append(lambda wslot=wslot, src=src, ti=ti: u_nf4(wslot, src, ti))
            for ptd in range(2):
                for ci in range(3):
                    p0_units.append(lambda ptd=ptd, ci=ci: u_nf4w3(ptd, ci))

            def u_ag0():
                agtok["ag0"] = p.do(
                    "gpsimd",
                    lambda E, S: E.collective_compute(
                        "AllGather", ALU.bypass, replica_groups=[list(range(8))],
                        ins=[ag0i.opt()], outs=[ag0o.opt()]),
                    reads=(B["ag0i"],), waits=(B["ag0i"].w,), inc="cc", serialize=True)
            p0_units.append(u_ag0)

            _ck(4)
            def u_tern(slot, src, scol, mode):
                cb = slot * L1_WSZ
                vop(lambda E, S, scol=scol: E.tensor_scalar(
                    out=stat[:, 22:23], in0=statB[:, scol:scol + 1], scalar1=0.5,
                    scalar2=None, op0=ALU.mult), reads=(B["statB"],), writes=(B["statTHR"],))
                vop(lambda E, S, scol=scol: E.tensor_scalar(
                    out=stat[:, 23:24], in0=statB[:, scol:scol + 1], scalar1=-0.5,
                    scalar2=None, op0=ALU.mult), reads=(B["statB"],), writes=(B["statTHR"],))
                nrt = (D if mode == "col" else F1P) // 128
                ncc = F1P if mode == "col" else D
                for ti in range(nrt):
                    dmaS(xt[:, 0:ncc], src[ti * 128:(ti + 1) * 128, :], writes=(B["xt"],))
                    vop(lambda E, S, ncc=ncc: E.tensor_scalar(
                        out=xnt[:, 0:ncc], in0=xt[:, 0:ncc], scalar1=stat[:, 22:23],
                        scalar2=1.0, op0=ALU.is_gt, op1=ALU.subtract),
                        reads=(B["xt"], B["statTHR"]), writes=(B["xnt"],))
                    tt = vop(lambda E, S, ncc=ncc: E.scalar_tensor_tensor(
                        out=dac[:, 0:ncc], in0=xt[:, 0:ncc], scalar=stat[:, 23:24],
                        in1=xnt[:, 0:ncc], op0=ALU.is_ge, op1=ALU.add),
                        reads=(B["xt"], B["statTHR"], B["xnt"]), writes=(B["dac"],))
                    if mode == "col":
                        for j in range(F1TPC):
                            dmaG(ag1i[cb + j * (D * 128):cb + (j + 1) * (D * 128)]
                                 .rearrange("(p a f) -> p a f", p=128, a=NDT)[:, ti, :],
                                 dac[:, j * 128:(j + 1) * 128],
                                 reads=(B["dac"],), writes=(B["ag1i"],))
                    else:
                        dmaG(ag1i[cb + ti * (128 * D):cb + (ti + 1) * (128 * D)]
                             .rearrange("(p f) -> p f", p=128), dac[:, 0:D],
                             reads=(B["dac"],), writes=(B["ag1i"],))

            for _t in range(NDT):
                p0_units.append(lambda t=_t: conv_tile(wo1, wo1b, t, dhh, B["dhh"],
                                                       B["wo1b"]))
            for _sl, _sr, _sc, _md in ((0, w1s1, 3, "col"), (1, w3s1, 4, "col"),
                                       (2, w2s1, 5, "row")):
                p0_units.append(lambda a=_sl, b=_sr, c=_sc, d=_md: u_tern(a, b, c, d))

            def u_ag1():
                agtok["ag1"] = p.do(
                    "gpsimd",
                    lambda E, S: E.collective_compute(
                        "AllGather", ALU.bypass, replica_groups=[list(range(8))],
                        ins=[ag1i.opt()], outs=[ag1o.opt()]),
                    reads=(B["ag1i"],), waits=(B["ag1i"].w,), inc="cc", serialize=True)
                merge_scratch_planes()
            p0_units.append(u_ag1)

            def merge_scratch_planes():
                for _nm, _pl in (("dsc", 13), ("dhh", 14), ("dac", 15)):
                    _b = B[_nm]
                    B_arA[_pl].rs.extend(_b.rs)
                    if _b.w:
                        B_arA[_pl].rs.append(_b.w)
                    _b.w = None
                    _b.rs = []

            def layernorm_pass(x_src, extra=()):
                for pt in range(NPT):
                    dmaS(xt[:, :], x_src[pt * 128:(pt + 1) * 128, :], writes=(B["xt"],),
                         waits=extra if pt == 0 else ())
                    vop(lambda E, S: E.reduce_sum(out=stat[:, 0:1], in_=xt[:, :], axis=AX.X),
                        reads=(B["xt"],), writes=(B["statLN"],))
                    aop(lambda E, S: E.activation(out=xnt[:, :], in_=xt[:, :], func=AF.Square,
                                                  accum_out=stat[:, 1:2]),
                        reads=(B["xt"],), writes=(B["xnt"], B["statLN"]))
                    vop(lambda E, S: E.tensor_scalar(out=stat[:, 2:3], in0=stat[:, 0:1],
                                                     scalar1=1.0 / D, scalar2=None,
                                                     op0=ALU.mult),
                        reads=(B["statLN"],), writes=(B["statLN"],))
                    vop(lambda E, S: E.tensor_scalar(out=stat[:, 3:4], in0=stat[:, 1:2],
                                                     scalar1=1.0 / D, scalar2=None,
                                                     op0=ALU.mult),
                        reads=(B["statLN"],), writes=(B["statLN"],))
                    vop(lambda E, S: E.tensor_scalar(out=stat[:, 4:5], in0=stat[:, 2:3],
                                                     scalar1=stat[:, 2:3], scalar2=None,
                                                     op0=ALU.mult),
                        reads=(B["statLN"],), writes=(B["statLN"],))
                    vop(lambda E, S: E.tensor_sub(out=stat[:, 3:4], in0=stat[:, 3:4],
                                                  in1=stat[:, 4:5]),
                        reads=(B["statLN"],), writes=(B["statLN"],))
                    aop(lambda E, S: E.activation(out=stat[:, 5:6], in_=stat[:, 3:4],
                                                  func=AF.Sqrt, bias=cst[:, 0:1]),
                        reads=(B["statLN"], B["cst"]), writes=(B["statLN"],))
                    vop(lambda E, S: E.reciprocal(out=stat[:, 6:7], in_=stat[:, 5:6]),
                        reads=(B["statLN"],), writes=(B["statLN"],))
                    vop(lambda E, S: E.tensor_scalar(out=xnt[:, :], in0=xt[:, :],
                                                     scalar1=stat[:, 2:3], scalar2=stat[:, 6:7],
                                                     op0=ALU.subtract, op1=ALU.mult),
                        reads=(B["xt"], B["statLN"]), writes=(B["xnt"],))
                    for j in range(NDT):
                        mm(lambda E, S, j=j: E.transpose(
                            ptr2[:, j % 2, 0:128], xnt[:, j * 128:(j + 1) * 128], idb[:, :]),
                           reads=(B["xnt"], B["idb"]), writes=(PTR2B[j % 2],), fin=True)
                        aop(lambda E, S, j=j, pt=pt: E.copy(
                            out=xnT[:, j, pt * 128:(pt + 1) * 128], in_=ptr2[:, j % 2, 0:128]),
                            reads=(PTR2B[j % 2],), writes=(B["xnT"],))

            def stream_w(dram_ap, hg, slot):
                view = wsB[:, slot, :].rearrange("p (a b) -> p a b", a=NDT)
                for half in range(2):
                    (dmaS if half == 0 else dmaG)(wsF[:, half, :],
                         dram_ap[hg, :, half * 1024:(half + 1) * 1024],
                         writes=(WSFB[half],))
                    aop(lambda E, S, half=half, slot=slot: E.copy(
                        out=wsB[:, slot, half * 1024:(half + 1) * 1024],
                        in_=wsF[:, half, :]),
                        reads=(WSFB[half],), writes=(WSBB[slot],))
                return view

            def conv_tile(srcT, dstT, t, stg, stgB, dstB):
                for half in range(2):
                    (dmaS if half == 0 else dmaG)(
                        wsF[:, half, :],
                        srcT[t * 128:(t + 1) * 128, half * 1024:(half + 1) * 1024],
                        writes=(WSFB[half],))
                    aop(lambda E, S, half=half, stg=stg: E.copy(
                        out=stg[:, half * 1024:(half + 1) * 1024], in_=wsF[:, half, :]),
                        reads=(WSFB[half],), writes=(stgB,))
                dmaG(dstT[t * 128:(t + 1) * 128, :], stg[:, :], reads=(stgB,),
                     writes=(dstB,))

            _ck(5)
            for _t in range(NDT):
                conv_tile(wo0, wo0b, _t, kTb, B["kTb"], B["wo0b"])
            layernorm_pass(x_in)

            avT = arA

            def proj_dk(view, slot, dst, dstb):
                for tc in range(4):
                    for z in range(NDT):
                        mm(lambda E, S, z=z, tc=tc, view=view: E.matmul(
                            pp[:, tc, :], view[:, z, :], xnT[:, z, tc * 512:(tc + 1) * 512],
                            start=(z == 0), stop=(z == NDT - 1)),
                           reads=(WSBB[slot], B["xnT"]), writes=(PPB[tc],),
                           fin=(z == NDT - 1))
                    vop(lambda E, S, tc=tc, dst=dst: E.tensor_copy(
                        out=dst[:, tc * 512:(tc + 1) * 512], in_=pp[:, tc, :]),
                        reads=(PPB[tc],), writes=(dstb,))

            _ck(6)
            for hg in range(16):
                if hg < 12:
                    for _ in range(4):
                        if p0_units:
                            p0_units.pop(0)()
                else:
                    while p0_units:
                        p0_units.pop(0)()
                vq = stream_w(wq0, hg, 0)
                proj_dk(vq, 0, qTb, B["qTb"])
                vk = stream_w(wk0, hg, 1)
                proj_dk(vk, 1, kTb, B["kTb"])
                vv = stream_w(wv0, hg, 2)
                for pt in range(NPT):
                    bank = pt % 4
                    for z in range(NDT):
                        mm(lambda E, S, z=z, pt=pt, bank=bank, vv=vv: E.matmul(
                            pp[:, bank, 0:128], xnT[:, z, pt * 128:(pt + 1) * 128],
                            vv[:, z, :], start=(z == 0), stop=(z == NDT - 1)),
                           reads=(WSBB[2], B["xnT"]), writes=(PPB[bank],),
                           fin=(z == NDT - 1))
                    vop(lambda E, S, pt=pt, bank=bank: E.tensor_copy(
                        out=vb[:, pt, :], in_=pp[:, bank, 0:128]),
                        reads=(PPB[bank],), writes=(B["vb"],))

                for hh in range(2):
                    off = 64 * hh

                    def sc_part(i, off=off, hg=hg):
                        par = i % 2
                        j = i % 4
                        nk = 128 if i == 0 else 256
                        if i == 0:
                            mm(lambda E, S, off=off: E.matmul(
                                psc[:, 0, 0:128], qTb[off:off + 64, 0:128],
                                kTb[off:off + 64, 0:128], start=True, stop=True),
                               reads=(B["qTb"], B["kTb"]), writes=(PSCB[0],), fin=True)
                        else:
                            mm(lambda E, S, off=off, i=i, par=par: E.matmul(
                                psc[:, par, 0:256], qTb[off:off + 64, i * 128:(i + 1) * 128],
                                kTb[off:off + 64, (i - 1) * 128:(i + 1) * 128],
                                start=True, stop=True),
                               reads=(B["qTb"], B["kTb"]), writes=(PSCB[par],), fin=True)
                        mrow = masks[:, 0, :] if i > 0 else masks[:, 1, 0:128]
                        vop(lambda E, S, par=par, nk=nk, mrow=mrow: E.scalar_tensor_tensor(
                            out=psc[:, par, 0:nk], in0=psc[:, par, 0:nk], scalar=0.125,
                            in1=mrow, op0=ALU.mult, op1=ALU.add),
                            reads=(B["masks"],), writes=(PSCB[par],))
                        aop(lambda E, S, par=par, j=j, nk=nk: E.activation(
                            out=pbb[:, j, 0:nk], in_=psc[:, par, 0:nk], func=AF.Exp,
                            accum_out=stat[:, 7 + j:8 + j]),
                            reads=(PSCB[par],), writes=(PBBB[j], SMB[j]))
                        vop(lambda E, S, j=j: E.reciprocal(
                            out=stat[:, 24 + j:25 + j], in_=stat[:, 7 + j:8 + j]),
                            reads=(SMB[j],), writes=(SMB[j],))
                        vop(lambda E, S, j=j, nk=nk: E.tensor_scalar(
                            out=pbb[:, j, 0:nk], in0=pbb[:, j, 0:nk],
                            scalar1=stat[:, 24 + j:25 + j], scalar2=None, op0=ALU.mult),
                            reads=(SMB[j],), writes=(PBBB[j],))

                    def tr_part(i, off=off, hg=hg):
                        j = i % 4
                        nk = 128 if i == 0 else 256
                        nkt = nk // 128
                        for kk in range(nkt):
                            mm(lambda E, S, j=j, kk=kk: E.transpose(
                                ptr[:, kk, 0:128], pbb[:, j, kk * 128:(kk + 1) * 128],
                                idb[:, :]),
                               reads=(PBBB[j], B["idb"]), writes=(PTRB[kk],), fin=True)
                            vop(lambda E, S, j=j, kk=kk: E.tensor_copy(
                                out=pTb[:, j, kk, :], in_=ptr[:, kk, 0:128]),
                                reads=(PTRB[kk],), writes=(PTBB[j],))

                    def mv_part(i, off=off, hg=hg):
                        par = i % 2
                        j = i % 4
                        nk = 128 if i == 0 else 256
                        nkt = nk // 128
                        for kk in range(nkt):
                            kpt = (i - 1 + kk) if i > 0 else 0
                            tpos = (0, off) if off else None
                            mm(lambda E, S, kpt=kpt, par=par, j=j, kk=kk, off=off,
                               tpos=tpos, st=(kk == 0), sp=(kk == nkt - 1): E.matmul(
                                pav[off:off + 64, par, 0:128], vb[:, kpt, off:off + 64],
                                pTb[:, j, kk, :], start=st, stop=sp, tile_position=tpos),
                               reads=(B["vb"], PTBB[j]), writes=(PAVB[par],),
                               fin=(kk == nkt - 1))
                        vop(lambda E, S, par=par, off=off, hg=hg, i=i: E.tensor_copy(
                            out=avT[off:off + 64, hg, i * 128:(i + 1) * 128],
                            in_=pav[off:off + 64, par, 0:128]),
                            reads=(PAVB[par],), writes=(B_arA[hg],))

                    sc_part(0)
                    sc_part(1)
                    tr_part(0)
                    for i in range(NB):
                        if i + 2 < NB:
                            sc_part(i + 2)
                        if i + 1 < NB:
                            tr_part(i + 1)
                        mv_part(i)


            def y_project(wT_dram, src_resid, dst_dram, dstB, bias_dram, scale_col):
                for dc in range(4):
                    if bias_dram is not None:
                        dmaG(brow[0:1, 0:512], bias_dram[0:1, dc * 512:(dc + 1) * 512],
                             writes=(B["brow"],))
                    for ptg in range(4):
                        for k in range(NDT):
                            par = k % 4
                            (dmaS if k % 2 == 0 else dmaG)(
                                wsC[:, par, :],
                                wT_dram[k * 128:(k + 1) * 128, dc * 512:(dc + 1) * 512],
                                writes=(WSCB[par],))
                            for ptl in range(4):
                                pt = ptg * 4 + ptl
                                last = (k == NDT - 1) and bias_dram is None
                                mm(lambda E, S, k=k, pt=pt, ptl=ptl, par=par,
                                   st=(k == 0), sp=last: E.matmul(
                                    pp[:, ptl, :], avT[:, k, pt * 128:(pt + 1) * 128],
                                    wsC[:, par, :], start=st, stop=sp),
                                   reads=(B_arA[k], WSCB[par]), writes=(PPB[ptl],),
                                   fin=(ptl == 3 or last))
                        if bias_dram is not None:
                            for ptl in range(4):
                                mm(lambda E, S, ptl=ptl: E.matmul(
                                    pp[:, ptl, :], onesb[0:1, :], brow[0:1, 0:512],
                                    start=False, stop=True),
                                   reads=(B["onesb"], B["brow"]), writes=(PPB[ptl],),
                                   fin=True)
                        for ptl in range(4):
                            pt = ptg * 4 + ptl
                            xb = ptl % 2
                            dmaS(xt2[:, xb, :],
                                 src_resid[pt * 128:(pt + 1) * 128, dc * 512:(dc + 1) * 512],
                                 writes=(XT2B[xb],))
                            vop(lambda E, S, ptl=ptl, xb=xb: E.tensor_add(
                                out=xt2[:, xb, :], in0=pp[:, ptl, :], in1=xt2[:, xb, :]),
                                reads=(PPB[ptl],), writes=(XT2B[xb],))
                            dmaG(dst_dram[pt * 128:(pt + 1) * 128, dc * 512:(dc + 1) * 512],
                                 xt2[:, xb, :], reads=(XT2B[xb],), writes=(dstB,))

            y_project(wo0b, x_in, xpd, B["xpd"], None, None)

            gview = arA[:, :, :].rearrange("p a b -> p (a b)")

            def ffn(ag_w_fn, ag_rhs_fn, nft, resid_dram, dst_dram, dstB, silu_scale,
                    out_scale, ag_tok):
                for tc in range(4):
                    for ft in range(nft):
                        fw = 128 if (nft == 44 or ft < 42) else F1 - 42 * 128
                        sa = (2 * ft) % 3
                        sb_ = (2 * ft + 1) % 3
                        dmaS(wsB[:, sa, :], ag_w_fn(0, ft), writes=(WSBB[sa],),
                             waits=(ag_tok,))
                        dmaG(wsB[:, sb_, :], ag_w_fn(1, ft), writes=(WSBB[sb_],),
                             waits=(ag_tok,))
                        va = wsB[:, sa, :].rearrange("p (a b) -> p a b", a=NDT)
                        vb_ = wsB[:, sb_, :].rearrange("p (a b) -> p a b", a=NDT)
                        for z in range(NDT):
                            mm(lambda E, S, z=z, tc=tc, fw=fw, v1=va: E.matmul(
                                pp[0:fw, 0, :], v1[:, z, 0:fw],
                                xnT[:, z, tc * 512:(tc + 1) * 512],
                                start=(z == 0), stop=(z == NDT - 1)),
                               reads=(WSBB[sa], B["xnT"]), writes=(PPB[0],),
                               fin=(z == NDT - 1))
                        if silu_scale is None:
                            aop(lambda E, S, fw=fw: E.activation(
                                out=xnt[0:fw, 0:512], in_=pp[0:fw, 0, :], func=AF.Silu),
                                reads=(PPB[0],), writes=(B["xnt"],))
                        else:
                            aop(lambda E, S, fw=fw: E.activation(
                                out=xnt[0:fw, 0:512], in_=pp[0:fw, 0, :], func=AF.Silu,
                                scale=silu_scale[0:fw]),
                                reads=(PPB[0], B["statB"]), writes=(B["xnt"],))
                        for z in range(NDT):
                            mm(lambda E, S, z=z, tc=tc, fw=fw, v1=vb_: E.matmul(
                                pp[0:fw, 1, :], v1[:, z, 0:fw],
                                xnT[:, z, tc * 512:(tc + 1) * 512],
                                start=(z == 0), stop=(z == NDT - 1)),
                               reads=(WSBB[sb_], B["xnT"]), writes=(PPB[1],),
                               fin=(z == NDT - 1))
                        vop(lambda E, S, ft=ft, fw=fw: E.tensor_tensor(
                            out=gview[0:fw, ft * 512:(ft + 1) * 512], in0=xnt[0:fw, 0:512],
                            in1=pp[0:fw, 1, :], op=ALU.mult),
                            reads=(B["xnt"], PPB[1]), writes=(B_arA[ft // 4],))
                    for dc in range(4):
                        for ft in range(nft):
                            fw = 128 if (nft == 44 or ft < 42) else F1 - 42 * 128
                            par = ft % 4
                            pn, srcs = ag_rhs_fn(ft, dc)
                            dmaF = dmaS if ft % 2 == 0 else dmaG
                            for dst_sl, src_ap in srcs:
                                dmaF(dst_sl, src_ap, writes=(WSCB[par],),
                                     waits=(ag_tok,))
                            for ptl in range(4):
                                mm(lambda E, S, ft=ft, ptl=ptl, fw=fw, par=par,
                                   st=(ft == 0), sp=(ft == nft - 1): E.matmul(
                                    pp[:, ptl, :],
                                    gview[0:fw, ft * 512 + ptl * 128:ft * 512 + ptl * 128 + 128],
                                    wsC[0:fw, par, :], start=st, stop=sp),
                                   reads=(B_arA[ft // 4], WSCB[par]), writes=(PPB[ptl],),
                                   fin=(ptl == 3 or ft == nft - 1))
                        for ptl in range(4):
                            pt = tc * 4 + ptl
                            xb = ptl % 2
                            dmaS(xt2[:, xb, :],
                                 resid_dram[pt * 128:(pt + 1) * 128, dc * 512:(dc + 1) * 512],
                                 reads=(B["xpd"],), writes=(XT2B[xb],))
                            if out_scale is None:
                                vop(lambda E, S, ptl=ptl, xb=xb: E.tensor_add(
                                    out=xt2[:, xb, :], in0=pp[:, ptl, :], in1=xt2[:, xb, :]),
                                    reads=(PPB[ptl],), writes=(XT2B[xb],))
                            else:
                                vop(lambda E, S, ptl=ptl, xb=xb: E.scalar_tensor_tensor(
                                    out=xt2[:, xb, :], in0=pp[:, ptl, :], scalar=out_scale,
                                    in1=xt2[:, xb, :], op0=ALU.mult, op1=ALU.add),
                                    reads=(PPB[ptl], B["statB"]), writes=(XT2B[xb],))
                            dmaG(dst_dram[pt * 128:(pt + 1) * 128, dc * 512:(dc + 1) * 512],
                                 xt2[:, xb, :], reads=(XT2B[xb],), writes=(dstB,))

            def ag0_w(ws, ft):
                sh, lt = ft // F0TPC, ft % F0TPC
                off = sh * L0_CON + ws * L0_W1SZ + lt * (D * 128)
                return ag0o[off:off + D * 128].rearrange("(p f) -> p f", p=128)

            def ag0_rhs(ft, dc):
                res = []
                for half in range(2):
                    sh = dc * 2 + half
                    off = sh * L0_CON + 2 * L0_W1SZ + ft * (128 * 256)
                    res.append((wsC[:, ft % 4, half * 256:(half + 1) * 256],
                                ag0o[off:off + 128 * 256].rearrange("(p f) -> p f", p=128)))
                return 1, res

            _ck(7)
            ffn(ag0_w, ag0_rhs, 44, xpd, x1d, B["x1d"], None, None, agtok["ag0"])

            _ck(8)
            layernorm_pass(x1d, extra=(B["x1d"].w,))

            _ck(9)
            dmaS(masks[:, 0, :], m1a_t[:, :], writes=(B["masks"],))
            dmaS(masks[:, 1, 0:128], m1b_t[:, :], writes=(B["masks"],))
            dmaS(chT[:, :, :].rearrange("p a b -> p (a b)"), cosh_t[:, :],
                 writes=(B["planes"],))
            dmaS(shT[:, :, :].rearrange("p a b -> p (a b)"), sinh_t[:, :],
                 writes=(B["planes"],))
            dmaS(frow[0:1, :], qn1[0:1, :], writes=(B["frow"],))
            mm(lambda E, S: E.matmul(pav[:, 0, 0:128], onesf[0:1, :], frow[0:1, :],
                                     start=True, stop=True),
               reads=(B["frow"], B["onesf"]), writes=(PAVB[0],), fin=True)
            vop(lambda E, S: E.tensor_copy(out=qnp[:, :], in_=pav[:, 0, 0:128]),
                reads=(PAVB[0],), writes=(B["qnp"],))
            dmaS(frow[0:1, :], kn1[0:1, :], writes=(B["frow"],))
            mm(lambda E, S: E.matmul(pav[:, 1, 0:128], onesf[0:1, :], frow[0:1, :],
                                     start=True, stop=True),
               reads=(B["frow"], B["onesf"]), writes=(PAVB[1],), fin=True)
            vop(lambda E, S: E.tensor_copy(out=knp[:, :], in_=pav[:, 1, 0:128]),
                reads=(PAVB[1],), writes=(B["knp"],))
            dmaS(arow[0:1, 15:16], sc1[0:1, :], writes=(B["arow"],))
            mm(lambda E, S: E.matmul(pav[:, 0, 0:1], onesf[0:1, :], arow[0:1, 15:16],
                                     start=True, stop=True),
               reads=(B["arow"], B["onesf"]), writes=(PAVB[0],), fin=True)
            vop(lambda E, S: E.tensor_copy(out=statB[:, 8:9], in_=pav[:, 0, 0:1]),
                reads=(PAVB[0],), writes=(B["statB"],))

            qtok = xt.rearrange("p (a b) -> p a b", a=NPT)
            rview = xnt.rearrange("p (a b) -> p a b", a=NPT)

            def l1_head(wT, b_dram, hgi, npl, nplB, dstT, dstB):
                col0 = hgi * 128
                view = stream_w(wT, hgi, 0)
                dmaG(brow[0:1, 0:128], b_dram[0:1, col0:col0 + 128], writes=(B["brow"],))
                for pt in range(NPT):
                    bank = pt % 4
                    for z in range(NDT):
                        mm(lambda E, S, z=z, pt=pt, bank=bank, view=view: E.matmul(
                            pp[:, bank, 0:128], xnT[:, z, pt * 128:(pt + 1) * 128],
                            view[:, z, :], start=(z == 0), stop=False),
                           reads=(WSBB[0], B["xnT"]), writes=(PPB[bank],))
                    mm(lambda E, S, bank=bank: E.matmul(
                        pp[:, bank, 0:128], onesb[0:1, :], brow[0:1, 0:128],
                        start=False, stop=True),
                       reads=(B["onesb"], B["brow"]), writes=(PPB[bank],), fin=True)
                    vop(lambda E, S, pt=pt, bank=bank: E.tensor_copy(
                        out=qtok[:, pt, :], in_=pp[:, bank, 0:128]),
                        reads=(PPB[bank],), writes=(B["xt"],))
                    aop(lambda E, S, pt=pt: E.activation(
                        out=rview[:, pt, :], in_=qtok[:, pt, :], func=AF.Square,
                        accum_out=stat[:, 14:15]),
                        reads=(B["xt"],), writes=(B["xnt"], B["statRMS"]))
                    aop(lambda E, S: E.activation(out=stat[:, 15:16], in_=stat[:, 14:15],
                                                  func=AF.Sqrt, scale=cst[:, 2:3],
                                                  bias=cst[:, 1:2]),
                        reads=(B["statRMS"], B["cst"]), writes=(B["statRMS"],))
                    vop(lambda E, S: E.reciprocal(out=stat[:, 16:17], in_=stat[:, 15:16]),
                        reads=(B["statRMS"],), writes=(B["statRMS"],))
                    vop(lambda E, S, pt=pt, npl=npl: E.scalar_tensor_tensor(
                        out=qtok[:, pt, :], in0=qtok[:, pt, :], scalar=stat[:, 16:17],
                        in1=npl, op0=ALU.mult, op1=ALU.mult),
                        reads=(B["statRMS"], nplB), writes=(B["xt"],))
                x1_, x2_ = qtok[:, :, 0:64], qtok[:, :, 64:128]
                o1_, o2_ = rview[:, :, 0:64], rview[:, :, 64:128]
                tA = wsF[:, 0, :].rearrange("p (a b) -> p a b", a=NPT)[:, :, 0:64]
                tB = wsF[:, 1, :].rearrange("p (a b) -> p a b", a=NPT)[:, :, 0:64]
                vop(lambda E, S: E.tensor_tensor(out=tA, in0=x1_, in1=chT[:, :, :], op=ALU.mult),
                    reads=(B["xt"], B["planes"]), writes=(WSFB[0],))
                vop(lambda E, S: E.tensor_tensor(out=tB, in0=x2_, in1=shT[:, :, :], op=ALU.mult),
                    reads=(B["xt"], B["planes"]), writes=(WSFB[1],))
                vop(lambda E, S: E.tensor_sub(out=tA, in0=tA, in1=tB),
                    reads=(WSFB[1],), writes=(WSFB[0],))
                vop(lambda E, S: E.tensor_copy(out=o1_, in_=tA),
                    reads=(WSFB[0],), writes=(B["xnt"],))
                vop(lambda E, S: E.tensor_tensor(out=tA, in0=x1_, in1=shT[:, :, :], op=ALU.mult),
                    reads=(B["xt"], B["planes"]), writes=(WSFB[0],))
                vop(lambda E, S: E.tensor_tensor(out=tB, in0=x2_, in1=chT[:, :, :], op=ALU.mult),
                    reads=(B["xt"], B["planes"]), writes=(WSFB[1],))
                vop(lambda E, S: E.tensor_add(out=tA, in0=tA, in1=tB),
                    reads=(WSFB[1],), writes=(WSFB[0],))
                vop(lambda E, S: E.tensor_copy(out=o2_, in_=tA),
                    reads=(WSFB[0],), writes=(B["xnt"],))
                for pt in range(NPT):
                    mm(lambda E, S, pt=pt: E.transpose(
                        ptr2[:, pt % 2, 0:128], xnt[:, pt * 128:(pt + 1) * 128], idb[:, :]),
                       reads=(B["xnt"], B["idb"]), writes=(PTR2B[pt % 2],), fin=True)
                    vop(lambda E, S, pt=pt, dstT=dstT: E.tensor_copy(
                        out=dstT[:, pt * 128:(pt + 1) * 128], in_=ptr2[:, pt % 2, 0:128]),
                        reads=(PTR2B[pt % 2],), writes=(dstB,))

            for kvh in range(L1KV):
                l1_head(wk1, bk1, kvh, knp[:, :], B["knp"], kTb, B["kTb"])
                vv = stream_w(wv1, kvh, 1)
                dmaG(brow[0:1, 128:256], bv1[0:1, kvh * 128:(kvh + 1) * 128],
                     writes=(B["brow"],))
                for pt in range(NPT):
                    bank = pt % 4
                    for z in range(NDT):
                        mm(lambda E, S, z=z, pt=pt, bank=bank, vv=vv: E.matmul(
                            pp[:, bank, 0:128], xnT[:, z, pt * 128:(pt + 1) * 128],
                            vv[:, z, :], start=(z == 0), stop=False),
                           reads=(WSBB[1], B["xnT"]), writes=(PPB[bank],))
                    mm(lambda E, S, bank=bank: E.matmul(
                        pp[:, bank, 0:128], onesb[0:1, :], brow[0:1, 128:256],
                        start=False, stop=True),
                       reads=(B["onesb"], B["brow"]), writes=(PPB[bank],), fin=True)
                    vop(lambda E, S, pt=pt, bank=bank: E.tensor_copy(
                        out=vb[:, pt, :], in_=pp[:, bank, 0:128]),
                        reads=(PPB[bank],), writes=(B["vb"],))
                for qq in range(4):
                    qh = kvh * 4 + qq
                    l1_head(wq1, bq1, qh, qnp[:, :], B["qnp"], qTb, B["qTb"])
                    def sc_part1(i, qh=qh):
                        par = i % 2
                        j = i % 4
                        nk = 128 if i == 0 else 256
                        if i == 0:
                            mm(lambda E, S: E.matmul(psc[:, 0, 0:128], qTb[:, 0:128],
                                                     kTb[:, 0:128], start=True, stop=True),
                               reads=(B["qTb"], B["kTb"]), writes=(PSCB[0],), fin=True)
                        else:
                            mm(lambda E, S, i=i, par=par: E.matmul(
                                psc[:, par, 0:256], qTb[:, i * 128:(i + 1) * 128],
                                kTb[:, (i - 1) * 128:(i + 1) * 128], start=True, stop=True),
                               reads=(B["qTb"], B["kTb"]), writes=(PSCB[par],), fin=True)
                        mrow = masks[:, 0, :] if i > 0 else masks[:, 1, 0:128]
                        vop(lambda E, S, par=par, nk=nk, mrow=mrow: E.scalar_tensor_tensor(
                            out=psc[:, par, 0:nk], in0=psc[:, par, 0:nk],
                            scalar=statB[:, 8:9], in1=mrow, op0=ALU.mult, op1=ALU.add),
                            reads=(B["masks"], B["statB"]), writes=(PSCB[par],))
                        aop(lambda E, S, par=par, j=j, nk=nk: E.activation(
                            out=pbb[:, j, 0:nk], in_=psc[:, par, 0:nk], func=AF.Exp,
                            accum_out=stat[:, 7 + j:8 + j]),
                            reads=(PSCB[par],), writes=(PBBB[j], SMB[j]))
                        vop(lambda E, S, j=j: E.reciprocal(
                            out=stat[:, 24 + j:25 + j], in_=stat[:, 7 + j:8 + j]),
                            reads=(SMB[j],), writes=(SMB[j],))
                        vop(lambda E, S, j=j, nk=nk: E.tensor_scalar(
                            out=pbb[:, j, 0:nk], in0=pbb[:, j, 0:nk],
                            scalar1=stat[:, 24 + j:25 + j], scalar2=None, op0=ALU.mult),
                            reads=(SMB[j],), writes=(PBBB[j],))

                    def tr_part1(i, qh=qh):
                        j = i % 4
                        nk = 128 if i == 0 else 256
                        nkt = nk // 128
                        for kk in range(nkt):
                            mm(lambda E, S, j=j, kk=kk: E.transpose(
                                ptr[:, kk, 0:128], pbb[:, j, kk * 128:(kk + 1) * 128],
                                idb[:, :]),
                               reads=(PBBB[j], B["idb"]), writes=(PTRB[kk],), fin=True)
                            vop(lambda E, S, j=j, kk=kk: E.tensor_copy(
                                out=pTb[:, j, kk, :], in_=ptr[:, kk, 0:128]),
                                reads=(PTRB[kk],), writes=(PTBB[j],))

                    def mv_part1(i, qh=qh):
                        par = i % 2
                        j = i % 4
                        nk = 128 if i == 0 else 256
                        nkt = nk // 128
                        for kk in range(nkt):
                            kpt = (i - 1 + kk) if i > 0 else 0
                            mm(lambda E, S, kpt=kpt, par=par, j=j, kk=kk,
                               st=(kk == 0), sp=(kk == nkt - 1): E.matmul(
                                pav[:, par, 0:128], vb[:, kpt, :], pTb[:, j, kk, :],
                                start=st, stop=sp),
                               reads=(B["vb"], PTBB[j]), writes=(PAVB[par],),
                               fin=(kk == nkt - 1))
                        vop(lambda E, S, par=par, qh=qh, i=i: E.tensor_copy(
                            out=avT[:, qh, i * 128:(i + 1) * 128], in_=pav[:, par, 0:128]),
                            reads=(PAVB[par],), writes=(B_arA[qh],))

                    sc_part1(0)
                    sc_part1(1)
                    tr_part1(0)
                    for i in range(NB):
                        if i + 2 < NB:
                            sc_part1(i + 2)
                        if i + 1 < NB:
                            tr_part1(i + 1)
                        mv_part1(i)

            y_project(wo1b, x1d, xpd, B["xpd"], bo1, None)

            def ag1_w(ws, ft):
                sh, lt = ft // F1TPC, ft % F1TPC
                off = sh * L1_CON + ws * L1_WSZ + lt * (D * 128)
                return ag1o[off:off + D * 128].rearrange("(p f) -> p f", p=128)

            def ag1_rhs(ft, dc):
                sh, lt = ft // F1TPC, ft % F1TPC
                off = sh * L1_CON + 2 * L1_WSZ + lt * (128 * D)
                fw = 128 if ft < 42 else F1 - 42 * 128
                return 1, [(wsC[0:fw, ft % 4, :],
                            ag1o[off:off + 128 * D].rearrange("(p f) -> p f", p=128)
                            [0:fw, dc * 512:(dc + 1) * 512])]

            _ck(10)
            ffn(ag1_w, ag1_rhs, NFT1, xpd, out, B["x1d"], statB[:, 3:4], statB[:, 6:7],
                agtok["ag1"])

        except _SB:
            pass

        for j in range(KQ):
            if p.cnt.get("qG%d" % j):
                p._wait("gpsimd", ("qG%d" % j, p.cnt["qG%d" % j]))
            if p.cnt.get("qS%d" % j):
                p._wait("sync", ("qS%d" % j, p.cnt["qS%d" % j]))

        import os
        for _e in ENGS:
            _tv = int(os.environ.get("TRUNC_" + _e, "0") or "0")
            if _tv:
                p.ops[_e] = p.ops[_e][:_tv]
        if os.environ.get("PRINTNV"):
            print("NVOPS", {e: len(p.ops[e]) for e in ENGS}, flush=True)

        with nc.Block() as block:
            @block.sync
            def _(E):
                for fn in p.ops["sync"]:
                    fn(E, sems)

            @block.scalar
            def _(E):
                for fn in p.ops["scalar"]:
                    fn(E, sems)

            @block.vector
            def _(E):
                for fn in p.ops["vector"]:
                    fn(E, sems)

            @block.tensor
            def _(E):
                for fn in p.ops["tensor"]:
                    fn(E, sems)

            @block.gpsimd
            def _(E):
                for fn in p.ops["gpsimd"]:
                    fn(E, sems)

    return nc


def kernel(L0_attn_x, L0_attn_Wq, L0_attn_Wk, L0_attn_Wv, L0_attn_Wo,
           L0_ffn_weight1, L0_ffn_weight2, L0_ffn_weight3,
           L1_attn_Wq, L1_attn_bq, L1_attn_Wk, L1_attn_bk, L1_attn_Wv, L1_attn_bv,
           L1_attn_Wo, L1_attn_bo, L1_attn_qn_w, L1_attn_kn_w, L1_attn_scale,
           L1_ffn_w1, L1_ffn_w3, L1_ffn_w2):
    perm = np.arange(128).reshape(64, 2).T.reshape(-1)
    permq = (np.arange(L1H)[:, None] * 128 + perm[None, :]).reshape(-1)
    permk = (np.arange(L1KV)[:, None] * 128 + perm[None, :]).reshape(-1)

    cT = lambda a: np.ascontiguousarray(np.asarray(a).T)
    w1p = np.zeros((8 * F0P, D), dtype=BFD)
    w1p[:F0] = np.asarray(L0_ffn_weight1)
    w2p = np.zeros((8 * F0P, D), dtype=BFD)
    w2p[:F0] = np.asarray(L0_ffn_weight2)
    w3f = np.asarray(L0_ffn_weight3)
    w1T1 = np.zeros((D, 8 * F1P), dtype=np.float32)
    w1T1[:, :F1] = np.asarray(L1_ffn_w1).T
    w3T1 = np.zeros((D, 8 * F1P), dtype=np.float32)
    w3T1[:, :F1] = np.asarray(L1_ffn_w3).T
    w2T1 = np.zeros((8 * F1P, D), dtype=np.float32)
    w2T1[:F1] = np.asarray(L1_ffn_w2).T

    def tile_cols(arrT):
        Din, ncols = arrT.shape
        nhg = ncols // 128
        o = np.empty((nhg, 128, Din), dtype=arrT.dtype)
        for hg in range(nhg):
            o[hg] = (arrT[:, hg * 128:(hg + 1) * 128]
                     .reshape(Din // 128, 128, 128).transpose(1, 0, 2)
                     .reshape(128, Din))
        return np.ascontiguousarray(o)

    shared = {
        "wq0": tile_cols(cT(L0_attn_Wq)), "wk0": tile_cols(cT(L0_attn_Wk)),
        "wv0": tile_cols(cT(L0_attn_Wv)),
        "wo0": cT(L0_attn_Wo),
        "wq1": tile_cols(np.ascontiguousarray(np.asarray(L1_attn_Wq)[permq].T)),
        "wk1": tile_cols(np.ascontiguousarray(np.asarray(L1_attn_Wk)[permk].T)),
        "wv1": tile_cols(cT(L1_attn_Wv)), "wo1": cT(L1_attn_Wo),
        "bq1": np.ascontiguousarray(np.asarray(L1_attn_bq)[permq].reshape(1, D)),
        "bk1": np.ascontiguousarray(np.asarray(L1_attn_bk)[permk].reshape(1, 512)),
        "bv1": np.ascontiguousarray(np.asarray(L1_attn_bv).reshape(1, 512)),
        "bo1": np.ascontiguousarray(np.asarray(L1_attn_bo).reshape(1, D)),
        "qn1": np.ascontiguousarray(np.asarray(L1_attn_qn_w)[perm].reshape(1, 128)),
        "kn1": np.ascontiguousarray(np.asarray(L1_attn_kn_w)[perm].reshape(1, 128)),
        "sc1": np.asarray(L1_attn_scale).reshape(1, 1).astype(np.float32),
    }
    in_maps = []
    for c in range(NCORE):
        m = dict(shared)
        m["x"] = np.ascontiguousarray(np.asarray(L0_attn_x)[c])
        m["w1s0"] = np.ascontiguousarray(w1p[c * F0P:(c + 1) * F0P])
        m["w2s0"] = np.ascontiguousarray(w2p[c * F0P:(c + 1) * F0P])
        m["w3s0"] = np.ascontiguousarray(w3f[c * (D // 8):(c + 1) * (D // 8)])
        m["w1s1"] = np.ascontiguousarray(w1T1[:, c * F1P:(c + 1) * F1P])
        m["w3s1"] = np.ascontiguousarray(w3T1[:, c * F1P:(c + 1) * F1P])
        m["w2s1"] = np.ascontiguousarray(w2T1[c * F1P:(c + 1) * F1P])
        in_maps.append(m)

    nc = build_nc()
    res = run_bass_kernel_spmd(nc, in_maps, core_ids=list(range(NCORE)))
    return np.stack([res.results[c]["out"] for c in range(NCORE)], axis=0)
